# version 14
# baseline (speedup 1.0000x reference)
import numpy as np
import ml_dtypes
from contextlib import ExitStack

import concourse.bass as bass
import concourse.mybir as mybir
from concourse.bass_utils import run_bass_kernel_spmd

F32 = mybir.dt.float32
BF16 = mybir.dt.bfloat16
AF = mybir.ActivationFunctionType
ALU = mybir.AluOpType
AX = mybir.AxisListType
NPBF16 = ml_dtypes.bfloat16

T = 8192
D = 1024
DI = 2048
IF = 2816
NKC = D // 128
NDC = DI // 128
NIC = IF // 128
TT1 = 512; NT1 = T // TT1
TT2 = 512; NT2 = T // TT2
TT3 = 512; NT3 = T // TT3
LC1 = TT1 // 128
LC2 = TT2 // 128
LC3 = TT3 // 128
EPS = 1e-5
ICG = [(0, 512), (512, 512), (1024, 512), (1536, 512), (2048, 512), (2560, 256)]
NG = len(ICG)
ICP = [(0, 1024), (1024, 1024), (2048, 768)]
GP_EVERY = 4

ENGS = ("sync", "tensor", "vector", "scalar", "gpsimd")


class S:
    __slots__ = ("h", "n")
    def __init__(self, h):
        self.h = h
        self.n = 0
    def inc(self, k=1):
        self.n += k
        return self.n


def emit_quant_scales(v, jobs):
    v.drain()
    for j in jobs:
        NB = j["FD"] // 16
        a3 = j["aab"].rearrange("p (nb b) -> p nb b", b=16)
        v.tensor_reduce(j["scr_h"][:, :NB], a3, AX.X, ALU.max)
    v.drain()
    for j in jobs:
        NB = j["FD"] // 16
        v.reciprocal(j["scr_r"][:, :NB], j["scr_h"][:, :NB])
    v.drain()
    for j in jobs:
        NB = j["FD"] // 16
        v.tensor_scalar(j["scr_r"][:, :NB], j["scr_r"][:, :NB], 6.0, 6.0e12,
                        ALU.mult, ALU.min)
        v.tensor_scalar(j["scr_h"][:, :NB], j["scr_h"][:, :NB], 1.0 / 12.0, 0.5e-12,
                        ALU.mult, ALU.max)
    v.drain()


def emit_quant_dense(e, jobs):
    for j in jobs:
        NB = j["FD"] // 16
        aab3 = j["aab"].rearrange("p (nb b) -> p nb b", b=16)
        rb = j["scr_r"][:, :NB].unsqueeze(2).broadcast_to([128, NB, 16])
        e.tensor_tensor(aab3, aab3, rb, ALU.mult)
    e.drain()
    for j in jobs:
        af, FD = j["aab"], j["FD"]
        e.tensor_single_scalar(j["cA"][:, :FD], af, 0.25, ALU.is_gt)
        e.tensor_single_scalar(j["cB"][:, :FD], af, 2.5, ALU.is_gt)
    e.drain()
    for tA, tB in ((0.75, 3.5), (1.25, 5.0), (1.75, 5.0)):
        for j in jobs:
            af, FD = j["aab"], j["FD"]
            e.scalar_tensor_tensor(j["cA"][:, :FD], af, tA, j["cA"][:, :FD],
                                   ALU.is_gt, ALU.add)
            e.scalar_tensor_tensor(j["cB"][:, :FD], af, tB, j["cB"][:, :FD],
                                   ALU.is_gt, ALU.add)
        e.drain()
    for j in jobs:
        FD = j["FD"]
        e.scalar_tensor_tensor(j["cB"][:, :FD], j["cB"][:, :FD], 2.0, j["cA"][:, :FD],
                               ALU.mult, ALU.add)
    e.drain()
    for j in jobs:
        FD = j["FD"]
        e.tensor_tensor(j["cB"][:, :FD], j["cB"][:, :FD], j["sgn"], ALU.mult)
    e.drain()
    last = None
    for j in jobs:
        NB = j["FD"] // 16
        hb = j["scr_h"][:, :NB].unsqueeze(2).broadcast_to([128, NB, 16])
        cB3 = j["cB"][:, :j["FD"]].rearrange("p (nb b) -> p nb b", b=16)
        o3 = j["out"].rearrange("p (nb b) -> p nb b", b=16)
        last = e.tensor_tensor(o3, cB3, hb, ALU.mult)
    return last


def build_nc():
    nc = bass.Bass("TRN2", target_bir_lowering=False)

    x_h = nc.dram_tensor("x", [T, D], F32, kind="ExternalInput").ap()
    w_in_h = nc.dram_tensor("w_in", [D, 4 * D], F32, kind="ExternalInput").ap()
    convw_h = nc.dram_tensor("conv_w", [DI, 4], F32, kind="ExternalInput").ap()
    convb_h = nc.dram_tensor("conv_b_t", [128, NDC], F32, kind="ExternalInput").ap()
    wxp_h = nc.dram_tensor("w_xproj", [DI, 33], F32, kind="ExternalInput").ap()
    dvec_h = nc.dram_tensor("d_t", [128, NDC], F32, kind="ExternalInput").ap()
    wout_h = nc.dram_tensor("w_out", [DI, D], F32, kind="ExternalInput").ap()
    wg_h = nc.dram_tensor("w_gate", [IF, D], F32, kind="ExternalInput").ap()
    wu_h = nc.dram_tensor("w_up", [IF, D], F32, kind="ExternalInput").ap()
    wd_h = nc.dram_tensor("w_down", [D, IF], F32, kind="ExternalInput").ap()
    eye_h = nc.dram_tensor("eye", [128, 128], BF16, kind="ExternalInput").ap()
    cst_h = nc.dram_tensor("consts", [128, 4], F32, kind="ExternalInput").ap()
    out_h = nc.dram_tensor("out", [T, D], F32, kind="ExternalOutput").ap()

    xc_sp = nc.dram_tensor("xc_sp", [NDC, 128, T], BF16, kind="Internal").ap()
    sz_sp = nc.dram_tensor("sz_sp", [NDC, 128, T], BF16, kind="Internal").ap()
    xq_sp = nc.dram_tensor("xq_sp", [NKC, 128, T], BF16, kind="Internal").ap()
    ln1_sp = nc.dram_tensor("ln1_sp", [T // 128, 128, D], BF16, kind="Internal").ap()
    win_sp = nc.dram_tensor("win_sp", [32, 128, 1024], BF16, kind="Internal").ap()
    wqg_sp = nc.dram_tensor("wqg_sp", [NKC, 128, IF], BF16, kind="Internal").ap()
    wqu_sp = nc.dram_tensor("wqu_sp", [NKC, 128, IF], BF16, kind="Internal").ap()
    wdT_sp = nc.dram_tensor("wdT_sp", [NIC, 128, D], BF16, kind="Internal").ap()

    Q = {e: [] for e in ENGS}
    marks = {}
    def em(e, fn):
        Q[e].append(fn)
    def mark(name):
        marks[name] = {e: len(Q[e]) for e in ENGS}

    stack = ExitStack()
    sems = {}
    def S_(name):
        if name not in sems:
            sems[name] = S(stack.enter_context(nc.semaphore(name)))
        return sems[name]

    def barrier(tag, dma_waits=()):
        if dma_waits:
            def fw(eng, dw=tuple(dma_waits)):
                for sem, val in dw:
                    eng.wait_ge(sem.h, val)
            em("sync", fw)
        b = S_("bar")
        target = b.n + 5
        for e in ENGS:
            def f(eng, b=b, target=target):
                eng.drain()
                eng.sem_inc(b.h, 1)
                eng.wait_ge(b.h, target)
            em(e, f)
        b.inc(5)

    with stack:
        pctx = stack.enter_context(ExitStack())
        def sb(ctx, name, shape, dt=F32):
            return ctx.enter_context(nc.sbuf_tensor(name, shape, dt))
        eye_b = sb(pctx, "eye_b", [128, 128], BF16)
        csts = sb(pctx, "csts", [128, 4], F32)
        convb_t = sb(pctx, "convb_t", [128, NDC], F32)
        d_t = sb(pctx, "d_t_s", [128, NDC], F32)
        p12 = ExitStack()
        Cm_all = sb(p12, "Cm_all", [16, T], BF16)
        h_acc = sb(p12, "h_acc", [16, DI], F32)

        sld = S_("sld")
        def f(e):
            e.dma_start(out=eye_b[:], in_=eye_h).then_inc(sld.h, 16)
            e.dma_start(out=csts[:], in_=cst_h).then_inc(sld.h, 16)
            e.dma_start(out=convb_t[:], in_=convb_h).then_inc(sld.h, 16)
            e.dma_start(out=d_t[:], in_=dvec_h).then_inc(sld.h, 16)
        em("sync", f); sld.inc(64)
        SMALL_LD = sld.n

        p01 = stack.enter_context(ExitStack())
        diag_b = sb(p01, "diag_b", [128, NDC * 4, 128], BF16)
        wxp_dB = sb(p01, "wxp_dB", [128, NDC, 20], BF16)
        wxp_C = sb(p01, "wxp_C", [128, NDC, 16], BF16)

        p0 = ExitStack()
        wf32 = sb(p0, "wf32", [128, 4 * D], F32)
        wcb = sb(p0, "wcb", [128, 4 * D], BF16)
        convw_s = sb(p0, "convw_s", [128, NDC, 4], F32)
        wxp_st = sb(p0, "wxp_st", [128, NDC, 33], F32)

        wld = S_("wld"); winc = S_("winc"); winsp = S_("winsp")

        def f(e):
            e.dma_start(out=convw_s[:],
                        in_=convw_h.rearrange("(c p) k -> p c k", p=128)
                        ).then_inc(wld.h, 16)
            e.dma_start(out=wxp_st[:].rearrange("p c n -> p c n"),
                        in_=wxp_h.rearrange("(c p) n -> p c n", p=128)
                        ).then_inc(wld.h, 16)
        em("sync", f); wld.inc(32)
        for kc in range(NKC):
            def f(e, kc=kc, c=winc.n):
                if kc > 0:
                    e.wait_ge(winc.h, c)
                e.dma_start(out=wf32[:],
                            in_=w_in_h[kc * 128:(kc + 1) * 128, :]).then_inc(wld.h, 16)
            em("sync", f); wld.inc(16)
            def f(e, kc=kc, w=wld.n, sp=winsp.n):
                e.wait_ge(wld.h, w)
                if kc > 0:
                    e.wait_ge(winsp.h, sp)
                e.drain()
                e.tensor_copy(wcb[:], wf32[:]).then_inc(winc.h, 1)
            em("vector", f); winc.inc(1)
            def f(e, kc=kc, c=winc.n):
                e.wait_ge(winc.h, c)
                e.dma_start(out=win_sp[:, :, kc * 128:(kc + 1) * 128]
                            .rearrange("oc p q -> p oc q"),
                            in_=wcb[:].rearrange("p (oc q) -> p oc q", q=128)
                            ).then_inc(winsp.h, 16)
            em("sync", f); winsp.inc(16)
        def f(e, w=wld.n, s=SMALL_LD):
            e.wait_ge(sld.h, s)
            e.wait_ge(wld.h, w)
            e.drain()
            wxpv = wxp_st[:]
            e.memset(wxp_dB[:], 0.0)
            e.drain()
            e.tensor_copy(wxp_dB[:, :, 0:17], wxpv[:, :, 0:17])
            e.tensor_copy(wxp_C[:], wxpv[:, :, 17:33])
            e.drain()
            for dc in range(NDC):
                for k in range(4):
                    e.tensor_scalar_mul(diag_b[:, dc * 4 + k, :], eye_b[:],
                                        convw_s[:, dc, k:k + 1])
            e.drain()
            e.sem_inc(winc.h, 1)
        em("vector", f); winc.inc(1)

        barrier("p0", [(wld, wld.n), (winsp, winsp.n), (sld, sld.n)])
        mark("p0")
        p0.close()

        p1 = ExitStack()
        winb = sb(p1, "winb", [128, 6, 1024], BF16)
        wst = [sb(p1, f"wst{i}", [128, 1024], F32) for i in range(2)]
        aabw = [sb(p1, f"aabw{i}", [128, 1024], F32) for i in range(2)]
        sgnw = [sb(p1, f"sgnw{i}", [128, 1024], BF16) for i in range(2)]
        wq_tm = [sb(p1, f"wqtm{i}", [128, 1024], BF16) for i in range(4)]
        wqT_st = [sb(p1, f"wqT{i}", [128, NKC, 128], BF16) for i in range(2)]
        scrw = {"r": sb(p1, "sw_r", [128, 64], F32),
                "h": sb(p1, "sw_h", [128, 64], F32),
                "cA": sb(p1, "sw_cA", [128, 1024], BF16),
                "cB": sb(p1, "sw_cB", [128, 1024], BF16)}
        xtm = [sb(p1, f"xtm{i}", [128, LC1, 1024], F32) for i in range(2)]
        xtm_b = sb(p1, "xtmb", [128, LC1, 1024], BF16)
        x_fm = sb(p1, "xfm", [128, NKC, TT1], BF16)
        xp_t = sb(p1, "xp_t", [128, NDC, TT1 + 3], BF16)
        sz_t = sb(p1, "sz_t", [128, NDC, TT1], BF16)
        xc_t = sb(p1, "xc_t", [128, NDC, TT1], BF16)
        xcT = [sb(p1, f"xcT{i}", [128, DI], BF16) for i in range(LC1)]
        ssm_s = sb(p1, "ssm_s", [17, TT1], BF16)
        ssm_tm = sb(p1, "ssm_tm", [128, LC1, 17], F32)
        dB_tm = sb(p1, "dB_tm", [128, LC1, 16], BF16)
        sp1t = sb(p1, "sp1", [1, TT1], F32)
        sp2t = sb(p1, "sp2", [1, TT1], F32)
        sp1 = sp1t[:]
        sp2 = sp2t[:]

        p1ps = ExitStack()
        ps_xz = [p1ps.enter_context(nc.psum_tensor(f"ps_xz{i}", [128, 512], F32))
                 for i in range(2)]
        ps_cv = [p1ps.enter_context(nc.psum_tensor(f"ps_cv{i}", [128, 512], F32))
                 for i in range(2)]
        ps_Ts = [p1ps.enter_context(nc.psum_tensor(f"ps_T{i}", [128, 1024], BF16))
                 for i in range(2)]
        ps_sA = p1ps.enter_context(nc.psum_tensor("ps_sA", [17, TT1], F32))
        ps_hm = p1ps.enter_context(nc.psum_tensor("ps_hm", [16, 512], F32))

        xld = S_("xld"); xcast = S_("xcast"); xfmT = S_("xfmT"); xfme = S_("xfme")
        xzmm = S_("xzmm"); xpev = S_("xpev"); zev = S_("zev"); haloev = S_("haloev")
        cvmm = S_("cvmm"); xcev = S_("xcev"); szsp = S_("szsp"); xcsp = S_("xcsp")
        ssmm = S_("ssmm"); sAev = S_("sAev"); spd = S_("spd"); spadd = S_("spadd")
        sBev = S_("sBev"); sTmm = S_("sTmm"); sTev = S_("sTev"); dBev = S_("dBev")
        xcTmm = S_("xcTmm"); xcTev = S_("xcTev"); hmm = S_("hmm"); hmev = S_("hmev")
        winl = [S_(f"winl{i}") for i in range(6)]
        wldw = [S_("wldw0"), S_("wldw1")]
        wabs = S_("wabs"); wqd = S_("wqd")
        wTm = S_("wTm"); wTev = S_("wTev")
        wspw = [S_("wspw0"), S_("wspw1")]

        psT_free = [None, None]
        psT_idx = [0]
        def psT_next():
            bk = psT_idx[0]
            psT_idx[0] ^= 1
            return bk

        witems = []
        for ch in range(NIC):
            witems.append((wg_h[ch * 128:(ch + 1) * 128, :], 1024, wqg_sp, ch, None))
        for ch in range(NIC):
            witems.append((wu_h[ch * 128:(ch + 1) * 128, :], 1024, wqu_sp, ch, None))
        for dm in range(NKC):
            for (off, w) in ((0, 1024), (1024, 1024), (2048, 768)):
                witems.append((wd_h[dm * 128:(dm + 1) * 128, off:off + w], w,
                               wdT_sp, dm, off))

        def emit_witem_front(j):
            src, W, dst, ch, off = witems[j]
            jb = j % 2
            def f(e, j=j, jb=jb, src=src, W=W, ab=wabs.n):
                if j >= 2:
                    e.wait_ge(wabs.h, ab - 1)
                e.dma_start(out=wst[jb][:, :W], in_=src).then_inc(wldw[jb].h, 16)
            em("gpsimd", f); wldw[jb].inc(16)
            def f(e, j=j, jb=jb, W=W, w=wldw[jb].n):
                e.wait_ge(wldw[jb].h, w)
                if j >= 2:
                    e.wait_ge(wqd.h, j - 1)
                e.drain()
                e.activation(aabw[jb][:, :W], wst[jb][:, :W], AF.Abs)
                e.drain()
                e.activation(sgnw[jb][:, :W], wst[jb][:, :W], AF.Sign)
                e.drain()
                e.sem_inc(wabs.h, 1)
            em("scalar", f); wabs.inc(1)
            def f(e, j=j, jb=jb, W=W, ab=wabs.n):
                e.wait_ge(wabs.h, ab)
                if j >= 4:
                    e.wait_ge(wTm.h, j - 3)
                jobs = [dict(aab=aabw[jb][:, :W], sgn=sgnw[jb][:, :W],
                             out=wq_tm[j % 4][:, :W], cA=scrw["cA"][:, :W],
                             cB=scrw["cB"][:, :W], scr_r=scrw["r"],
                             scr_h=scrw["h"], FD=W)]
                emit_quant_scales(e, jobs)
                emit_quant_dense(e, jobs)
                e.drain()
                e.sem_inc(wqd.h, 1)
            em("vector", f); wqd.inc(1)

        def emit_witem_back(j):
            src, W, dst, ch, off = witems[j]
            nt = W // 128
            jb = j % 2
            bk = psT_next()
            def f(e, j=j, nt=nt, bk=bk, ptf=psT_free[bk]):
                e.wait_ge(wqd.h, j + 1)
                if ptf is not None:
                    e.wait_ge(ptf[0].h, ptf[1])
                last = None
                for i in range(nt):
                    last = e.matmul(ps_Ts[bk][:, i * 128:(i + 1) * 128],
                                    wq_tm[j % 4][:, i * 128:(i + 1) * 128], eye_b[:],
                                    is_transpose=True, start=(i == 0),
                                    stop=(i == nt - 1))
                last.then_inc(wTm.h, 1)
            em("tensor", f); wTm.inc(1)
            def f(e, j=j, jb=jb, nt=nt, bk=bk):
                e.wait_ge(wTm.h, j + 1)
                if j >= 2:
                    e.wait_ge(wspw[jb].h, 16 * (j // 2))
                e.drain()
                e.activation(wqT_st[jb][:, :nt, :],
                             ps_Ts[bk][:, :nt * 128].rearrange("p (i q) -> p i q",
                                                               q=128),
                             AF.Identity).then_inc(wTev.h, 1)
            em("scalar", f); wTev.inc(1)
            psT_free[bk] = (wTev, wTev.n)
            if off is None:
                dhbm = dst[:, :, ch * 128:(ch + 1) * 128]
            else:
                dhbm = dst[off // 128:off // 128 + nt, :, ch * 128:(ch + 1) * 128]
            def f(e, j=j, jb=jb, nt=nt, dhbm=dhbm):
                e.wait_ge(wTev.h, j + 1)
                e.dma_start(out=dhbm.rearrange("f p q -> p f q"),
                            in_=wqT_st[jb][:, :nt, :]).then_inc(wspw[jb].h, 16)
            em("gpsimd", f); wspw[jb].inc(16)

        NWI = len(witems)
        wi_per_tile = [5] * 4 + [4] * 12
        assert sum(wi_per_tile) == NWI
        wi_next = [0]
        wi_done = [0]
        def wi_front(frac, t):
            target = sum(wi_per_tile[:t]) + int(round(wi_per_tile[t] * frac))
            while wi_next[0] < min(target, NWI):
                emit_witem_front(wi_next[0])
                wi_next[0] += 1
        def wi_back(t):
            target = sum(wi_per_tile[:t + 1])
            while wi_done[0] < min(target, wi_next[0]):
                emit_witem_back(wi_done[0])
                wi_done[0] += 1

        def f(e):
            e.memset(h_acc[:], 0.0)
            e.drain()
            e.memset(xp_t[:, :, 0:3], 0.0)
            e.drain()
            e.sem_inc(haloev.h, 1)
        em("vector", f); haloev.inc(1)

        for t in range(NT1):
            pp = t % 2
            xpev0, zev0, xcev0 = xpev.n, zev.n, xcev.n
            def f(e, t=t, xc_=xcast.n):
                if t > 1:
                    e.wait_ge(xcast.h, xc_ - 1)
                e.dma_start(out=xtm[t % 2][:],
                            in_=x_h[t * TT1:(t + 1) * TT1, :].rearrange(
                                "(c p) d -> p c d", p=128)).then_inc(xld.h, 16)
            em("sync", f); xld.inc(16)
            def f(e, t=t, w=xld.n, fT=xfmT.n):
                e.wait_ge(xld.h, w)
                if t >= 1:
                    e.wait_ge(xfmT.h, fT)
                e.drain()
                e.activation(xtm_b[:], xtm[t % 2][:], AF.Identity
                             ).then_inc(xcast.h, 1)
            em("scalar", f); xcast.inc(1)
            for c in range(LC1):
                bk = psT_next()
                def f(e, t=t, c=c, bk=bk, xc_=xcast.n, ptf=psT_free[bk]):
                    if c == 0:
                        e.wait_ge(xcast.h, xc_)
                    if ptf is not None:
                        e.wait_ge(ptf[0].h, ptf[1])
                    last = None
                    for i in range(NKC):
                        last = e.matmul(ps_Ts[bk][:, i * 128:(i + 1) * 128],
                                 xtm_b[:, c, i * 128:(i + 1) * 128], eye_b[:],
                                 is_transpose=True, start=(i == 0), stop=(i == NKC - 1))
                    last.then_inc(xfmT.h, 1)
                em("tensor", f); xfmT.inc(1)
                def f(e, t=t, c=c, bk=bk, m=xfmT.n):
                    e.wait_ge(xfmT.h, m)
                    e.drain()
                    e.activation(x_fm[:, :, c * 128:(c + 1) * 128],
                                 ps_Ts[bk][:].rearrange("p (i q) -> p i q", q=128),
                                 AF.Identity).then_inc(xfme.h, 1)
                em("scalar", f); xfme.inc(1)
                psT_free[bk] = (xfme, xfme.n)
            def f(e, t=t, cm=cvmm.n):
                e.wait_ge(cvmm.h, cm)
                if t > 0:
                    e.drain()
                    e.tensor_copy(xp_t[:, :, 0:3], xp_t[:, :, TT1:TT1 + 3])
                e.drain()
                e.sem_inc(haloev.h, 1)
            em("vector", f); haloev.inc(1)
            for oc in range(32):
                g = 32 * t + oc
                def f(e, g=g, oc=oc):
                    if g >= 5:
                        e.wait_ge(xzmm.h, g - 5)
                    e.dma_start(out=winb[:, g % 6:g % 6 + 1, :],
                                in_=win_sp[oc:oc + 1].rearrange("o p q -> p o q")
                                ).then_inc(winl[g % 6].h, 16)
                em("sync", f); winl[g % 6].inc(16)
            for oc in range(32):
                g = 32 * t + oc
                def f(e, t=t, oc=oc, g=g, fe=xfme.n, xp0=xpev0, zz0=zev0):
                    if oc == 0:
                        e.wait_ge(xfme.h, fe)
                    e.wait_ge(winl[g % 6].h, 16 * (g // 6 + 1))
                    if oc >= 2:
                        if oc - 2 < 16:
                            e.wait_ge(xpev.h, xp0 + oc - 1)
                        else:
                            e.wait_ge(zev.h, zz0 + oc - 17)
                    elif t > 0:
                        e.wait_ge(zev.h, zz0 + oc - 1)
                    last = None
                    for kc in range(NKC):
                        last = e.matmul(ps_xz[oc % 2][:],
                                 winb[:, g % 6, kc * 128:(kc + 1) * 128],
                                 x_fm[:, kc, :],
                                 start=(kc == 0), stop=(kc == NKC - 1))
                    last.then_inc(xzmm.h, 1)
                em("tensor", f); xzmm.inc(1)
                if oc < 16:
                    def f(e, t=t, oc=oc, m=xzmm.n, hl=haloev.n):
                        e.wait_ge(xzmm.h, m)
                        if oc == 0:
                            e.wait_ge(haloev.h, hl)
                        e.drain()
                        e.activation(xp_t[:, oc, 3:TT1 + 3], ps_xz[oc % 2][:],
                                     AF.Identity).then_inc(xpev.h, 1)
                    em("scalar", f); xpev.inc(1)
                else:
                    zc = oc - 16
                    def f(e, t=t, oc=oc, zc=zc, m=xzmm.n, sp_=szsp.n):
                        e.wait_ge(xzmm.h, m)
                        if zc == 0 and t > 0:
                            e.wait_ge(szsp.h, sp_)
                        e.drain()
                        e.activation(sz_t[:, zc, :], ps_xz[oc % 2][:], AF.Silu
                                     ).then_inc(zev.h, 1)
                    em("scalar", f); zev.inc(1)
            wi_front(0.25, t)
            mark(f"p1t{t}xz")
            def f(e, t=t, z=zev.n):
                e.wait_ge(zev.h, z)
                e.dma_start(out=sz_sp[:, :, t * TT1:(t + 1) * TT1]
                            .rearrange("c p s -> p c s"),
                            in_=sz_t[:]).then_inc(szsp.h, 16)
            em("gpsimd", f); szsp.inc(16)
            for dc in range(NDC):
                def f(e, t=t, dc=dc, xp0=xpev0, xc0=xcev0, hl=haloev.n):
                    e.wait_ge(xpev.h, xp0 + dc + 1)
                    if dc == 0:
                        e.wait_ge(haloev.h, hl)
                    if t > 0 or dc >= 2:
                        e.wait_ge(xcev.h, xc0 + dc - 1)
                    last = None
                    for k in range(4):
                        last = e.matmul(ps_cv[dc % 2][:], diag_b[:, dc * 4 + k, :],
                                 xp_t[:, dc, k:k + TT1],
                                 start=(k == 0), stop=(k == 3))
                    last.then_inc(cvmm.h, 1)
                em("tensor", f); cvmm.inc(1)
                def f(e, t=t, dc=dc, m=cvmm.n, sp_=xcsp.n, sm=ssmm.n, xT=xcTmm.n):
                    e.wait_ge(cvmm.h, m)
                    if dc == 0 and t > 0:
                        e.wait_ge(xcsp.h, sp_)
                        e.wait_ge(ssmm.h, sm)
                        e.wait_ge(xcTmm.h, xT)
                    e.drain()
                    e.activation(xc_t[:, dc, :], ps_cv[dc % 2][:], AF.Silu,
                                 bias=convb_t[:, dc:dc + 1]).then_inc(xcev.h, 1)
                em("scalar", f); xcev.inc(1)
            wi_front(0.5, t)
            mark(f"p1t{t}conv")
            def f(e, t=t, xe=xcev.n):
                e.wait_ge(xcev.h, xe)
                e.dma_start(out=xc_sp[:, :, t * TT1:(t + 1) * TT1]
                            .rearrange("c p s -> p c s"),
                            in_=xc_t[:]).then_inc(xcsp.h, 16)
            em("gpsimd", f); xcsp.inc(16)
            mark(f"p1t{t}xcsp")
            def f(e, t=t, xe=xcev.n, ae=sAev.n, sd_=spd.n):
                e.wait_ge(xcev.h, xe)
                if t > 0:
                    e.wait_ge(sAev.h, ae)
                    e.wait_ge(spd.h, sd_)
                last = None
                for dc in range(NDC):
                    last = e.matmul(ps_sA[:], wxp_dB[:, dc, 0:17], xc_t[:, dc, :],
                             start=(dc == 0), stop=(dc == NDC - 1))
                last.then_inc(ssmm.h, 1)
            em("tensor", f); ssmm.inc(1)
            def f(e, m=ssmm.n, st_=sTmm.n, t=t):
                e.wait_ge(ssmm.h, m)
                if t > 0:
                    e.wait_ge(sTmm.h, st_)
                e.drain()
                e.tensor_copy(ssm_s[:], ps_sA[:]).then_inc(sAev.h, 1)
            em("vector", f); sAev.inc(1)
            mark(f"p1t{t}ssmA")
            def f(e, ae=sAev.n):
                e.wait_ge(sAev.h, ae)
                e.drain()
                e.activation(sp1, ssm_s[0:1, :], AF.Abs)
                e.drain()
                e.activation(sp1, sp1, AF.Exp, scale=-1.0)
                e.drain()
                e.activation(sp1, sp1, AF.Ln, bias=csts[0:1, 1:2])
                e.drain()
                e.activation(sp2, ssm_s[0:1, :], AF.Relu)
                e.drain()
                e.sem_inc(spd.h, 1)
            em("scalar", f); spd.inc(1)
            def f(e, s=spd.n):
                e.wait_ge(spd.h, s)
                e.drain()
                e.tensor_tensor(ssm_s[0:1, :], sp1, sp2, ALU.add
                                ).then_inc(spadd.h, 1)
            em("vector", f); spadd.inc(1)
            wi_front(0.75, t)
            mark(f"p1t{t}sp")
            def f(e, t=t, be=sBev.n, hm_=hmev.n):
                if t > 0:
                    e.wait_ge(sBev.h, be)
                    e.wait_ge(hmev.h, hm_)
                last = None
                for dc in range(NDC):
                    last = e.matmul(ps_hm[:], wxp_C[:, dc, :], xc_t[:, dc, :],
                             start=(dc == 0), stop=(dc == NDC - 1))
                last.then_inc(ssmm.h, 1)
            em("tensor", f); ssmm.inc(1)
            def f(e, t=t, m=ssmm.n):
                e.wait_ge(ssmm.h, m)
                e.drain()
                e.tensor_copy(Cm_all[:, t * TT1:(t + 1) * TT1], ps_hm[:]
                              ).then_inc(sBev.h, 1)
            em("vector", f); sBev.inc(1)
            mark(f"p1t{t}ssm")
            bk = psT_next()
            def f(e, bk=bk, ae=sAev.n, sa=spadd.n, ptf=psT_free[bk]):
                e.wait_ge(sAev.h, ae)
                e.wait_ge(spadd.h, sa)
                if ptf is not None:
                    e.wait_ge(ptf[0].h, ptf[1])
                last = None
                for lc in range(LC1):
                    last = e.matmul(ps_Ts[bk][:, lc * 20:lc * 20 + 17],
                             ssm_s[:, lc * 128:(lc + 1) * 128], eye_b[0:17, 0:17],
                             is_transpose=True, start=(lc == 0), stop=(lc == LC1 - 1))
                last.then_inc(sTmm.h, 1)
            em("tensor", f); sTmm.inc(1)
            def f(e, bk=bk, m=sTmm.n):
                e.wait_ge(sTmm.h, m)
                e.drain()
                e.tensor_copy(ssm_tm[:],
                              ps_Ts[bk][:, :LC1 * 20].rearrange("p (c n) -> p c n",
                                                                n=20)
                              [:, :, 0:17]).then_inc(sTev.h, 1)
            em("vector", f); sTev.inc(1)
            psT_free[bk] = (sTev, sTev.n)
            def f(e, m=sTev.n, hm_=hmm.n, t=t):
                e.wait_ge(sTev.h, m)
                if t > 0:
                    e.wait_ge(hmm.h, hm_)
                for lc in range(LC1):
                    e.drain()
                    e.tensor_scalar_mul(dB_tm[:, lc, :], ssm_tm[:, lc, 1:17],
                                        ssm_tm[:, lc, 0:1])
                e.drain()
                e.sem_inc(dBev.h, 1)
            em("vector", f); dBev.inc(1)
            for lc in range(LC1):
                for b in range(2):
                    bk = psT_next()
                    def f(e, t=t, lc=lc, b=b, bk=bk, hm_=hmm.n, ptf=psT_free[bk]):
                        if ptf is not None:
                            e.wait_ge(ptf[0].h, ptf[1])
                        if lc == 0 and b == 0 and t > 0:
                            e.wait_ge(hmm.h, hm_)
                        last = None
                        for i in range(8):
                            dc = b * 8 + i
                            last = e.matmul(ps_Ts[bk][:, i * 128:(i + 1) * 128],
                                     xc_t[:, dc, lc * 128:(lc + 1) * 128], eye_b[:],
                                     is_transpose=True, start=(i == 0), stop=(i == 7))
                        last.then_inc(xcTmm.h, 1)
                    em("tensor", f); xcTmm.inc(1)
                    def f(e, lc=lc, b=b, bk=bk, m=xcTmm.n):
                        e.wait_ge(xcTmm.h, m)
                        e.drain()
                        e.activation(xcT[lc][:, b * 1024:(b + 1) * 1024],
                                     ps_Ts[bk][:], AF.Identity).then_inc(xcTev.h, 1)
                    em("scalar", f); xcTev.inc(1)
                    psT_free[bk] = (xcTev, xcTev.n)
            mark(f"p1t{t}trans")
            for q in range(4):
                def f(e, t=t, q=q, dB=dBev.n, xT=xcTev.n, he=hmev.n, be=sBev.n):
                    if q == 0:
                        e.wait_ge(dBev.h, dB)
                        e.wait_ge(xcTev.h, xT)
                        e.wait_ge(sBev.h, be)
                    if he >= 1:
                        e.wait_ge(hmev.h, he)
                    last = None
                    for lc in range(LC1):
                        last = e.matmul(ps_hm[:], dB_tm[:, lc, :],
                                 xcT[lc][:, q * 512:(q + 1) * 512],
                                 start=(lc == 0), stop=(lc == LC1 - 1))
                    last.then_inc(hmm.h, 1)
                em("tensor", f); hmm.inc(1)
                def f(e, q=q, m=hmm.n):
                    e.wait_ge(hmm.h, m)
                    e.drain()
                    e.tensor_tensor(h_acc[:, q * 512:(q + 1) * 512],
                                    h_acc[:, q * 512:(q + 1) * 512], ps_hm[:],
                                    ALU.add).then_inc(hmev.h, 1)
                em("vector", f); hmev.inc(1)

            wi_front(1.0, t)
            wi_back(t)
            mark(f"p1t{t}")
        assert wi_next[0] == NWI and wi_done[0] == NWI
        barrier("p1", [(szsp, szsp.n), (xcsp, xcsp.n), (xld, xld.n),
                       (wspw[0], wspw[0].n), (wspw[1], wspw[1].n),
                       (wldw[0], wldw[0].n), (wldw[1], wldw[1].n)])
        mark("p1")
        p1.close()
        p1ps.close()
        p01.close()

        p2 = ExitStack()
        wout_b = sb(p2, "wout_b", [128, NDC, D], BF16)
        hmb = sb(p2, "hmb", [16, DI], BF16)
        xcv = sb(p2, "xcv", [128, NDC, TT2], BF16)
        szv = sb(p2, "szv", [128, NDC, TT2], BF16)
        y1 = sb(p2, "y1", [128, NDC, TT2], BF16)
        xstm = [sb(p2, f"xstm{i}", [128, LC2, 1024], F32) for i in range(2)]
        wout_st = xstm[0][:, 0, :]
        resid = sb(p2, "resid", [128, LC2, 1024], F32)
        ln1b = sb(p2, "ln1b", [128, LC2, 1024], BF16)
        xq_tm = sb(p2, "xq_tm", [128, LC2, 1024], BF16)
        xq_fm = sb(p2, "xq_fm", [128, NKC, TT2], BF16)
        sq_scr = sb(p2, "sq_scr", [128, 1024], BF16)
        aab2 = sb(p2, "aab2", [128, 2048], F32)
        sgn2 = sb(p2, "sgn2", [128, 2048], BF16)
        st_s = sb(p2, "st_s", [128, LC2], F32)
        st_q = sb(p2, "st_q", [128, LC2], F32)
        st_m = sb(p2, "st_m", [128, LC2], F32)
        st_v = sb(p2, "st_v", [128, LC2], F32)
        st_r = sb(p2, "st_r", [128, LC2], F32)
        st_b = sb(p2, "st_b", [128, LC2], F32)
        scr2 = {k: sb(p2, "s2_" + k, [128, 2048 if k not in ("r", "h") else 128],
                      F32 if k in ("r", "h") else BF16)
                for k in ("r", "h", "cA", "cB")}

        p2ps = ExitStack()
        ps_y = [p2ps.enter_context(nc.psum_tensor(f"ps_y{i}", [128, TT2], F32))
                for i in range(2)]
        ps_at = [p2ps.enter_context(nc.psum_tensor(f"ps_at{i}", [128, 1024], F32))
                 for i in range(2)]
        ps_xqT = p2ps.enter_context(nc.psum_tensor("ps_xqT", [128, 1024], BF16))

        wol = S_("wol"); woc = S_("woc"); hmc = S_("hmc")
        xcvl = S_("xcvl"); szvl = S_("szvl")
        xsl = [S_("xsl0"), S_("xsl1")]
        ymm = S_("ymm"); ypev = S_("ypev"); womm = S_("womm"); rsev = S_("rsev")
        sesq = S_("sesq"); stat = S_("stat"); seap = S_("seap")
        qx = S_("qx"); xqTm = S_("xqTm"); xqTe = S_("xqTe")
        lnsps = S_("lnsps"); xqsps = S_("xqsps")

        for dc in range(NDC):
            def f(e, dc=dc, c=woc.n):
                if dc > 0:
                    e.wait_ge(woc.h, c)
                e.dma_start(out=wout_st, in_=wout_h[dc * 128:(dc + 1) * 128, :]
                            ).then_inc(wol.h, 16)
            em("sync", f); wol.inc(16)
            def f(e, dc=dc, w=wol.n):
                e.wait_ge(wol.h, w)
                e.drain()
                e.tensor_copy(wout_b[:, dc, :], wout_st).then_inc(woc.h, 1)
            em("vector", f); woc.inc(1)
        def f(e):
            e.drain()
            e.tensor_scalar_mul(hmb[:], h_acc[:], 1.0 / T)
            e.drain()
            e.sem_inc(hmc.h, 1)
        em("vector", f); hmc.inc(1)

        pend_q2a = []
        pend_t2a = []
        for t in range(NT2):
            pend_e2a = []
            pp = t % 2
            tok0 = t * TT2
            ypev0, rsev0 = ypev.n, rsev.n
            def f(e, t=t, yp=ypev.n, tok0=tok0):
                if t >= 1:
                    e.wait_ge(ypev.h, yp)
                e.dma_start(out=xcv[:],
                            in_=xc_sp[:, :, tok0:tok0 + TT2].rearrange("c p s -> p c s")
                            ).then_inc(xcvl.h, 16)
            em("sync", f); xcvl.inc(16)
            def f(e, t=t, yp=ypev.n, tok0=tok0):
                if t >= 1:
                    e.wait_ge(ypev.h, yp)
                e.dma_start(out=szv[:],
                            in_=sz_sp[:, :, tok0:tok0 + TT2].rearrange("c p s -> p c s")
                            ).then_inc(szvl.h, 16)
            em("sync", f); szvl.inc(16)
            def f(e, t=t, pp=pp, rs=rsev.n, wc=woc.n, tok0=tok0):
                if t == 0:
                    e.wait_ge(woc.h, wc)
                if t >= 2:
                    e.wait_ge(rsev.h, rs - LC2)
                e.dma_start(out=xstm[pp][:],
                            in_=x_h[tok0:tok0 + TT2, :].rearrange("(c p) d -> p c d",
                                                                  p=128)
                            ).then_inc(xsl[pp].h, 16)
            em("sync", f); xsl[pp].inc(16)
            for dc in range(NDC):
                def f(e, t=t, dc=dc, hc=hmc.n, yp0=ypev0, tok0=tok0):
                    if t == 0 and dc == 0:
                        e.wait_ge(hmc.h, hc)
                    if t > 0 or dc >= 2:
                        e.wait_ge(ypev.h, yp0 + dc - 1)
                    e.matmul(ps_y[dc % 2][:], hmb[:, dc * 128:(dc + 1) * 128],
                             Cm_all[:, tok0:tok0 + TT2], start=True, stop=True
                             ).then_inc(ymm.h, 1)
                em("tensor", f); ymm.inc(1)
                def f(e, t=t, pp=pp, dc=dc, m=ymm.n, xl=xcvl.n, zl=szvl.n, wm=womm.n):
                    e.wait_ge(ymm.h, m)
                    if dc == 0:
                        e.wait_ge(xcvl.h, xl)
                        e.wait_ge(szvl.h, zl)
                        if t > 0:
                            e.wait_ge(womm.h, wm)
                    e.drain()
                    e.scalar_tensor_tensor(y1[:, dc, :], xcv[:, dc, :], d_t[:, dc:dc + 1],
                                           ps_y[dc % 2][:], ALU.mult, ALU.add)
                    e.drain()
                    e.tensor_tensor(y1[:, dc, :], y1[:, dc, :], szv[:, dc, :], ALU.mult
                                    ).then_inc(ypev.h, 1)
                em("vector", f); ypev.inc(1)
            for fn in pend_q2a:
                em("vector", fn)
            pend_q2a = []
            for lc in range(LC2):
                def f(e, t=t, lc=lc, yp=ypev.n, rs0=rsev0):
                    if lc == 0:
                        e.wait_ge(ypev.h, yp)
                    if t > 0 or lc >= 2:
                        e.wait_ge(rsev.h, rs0 + lc - 1)
                    last = None
                    for half in range(2):
                        for dc in range(NDC):
                            last = e.matmul(ps_at[lc % 2][:, half * 512:(half + 1) * 512],
                                     y1[:, dc, lc * 128:(lc + 1) * 128],
                                     wout_b[:, dc, half * 512:(half + 1) * 512],
                                     start=(dc == 0), stop=(dc == NDC - 1))
                    last.then_inc(womm.h, 1)
                em("tensor", f); womm.inc(1)
                def f(e, t=t, pp=pp, lc=lc, m=womm.n, xl=xsl[pp].n, sa=seap.n):
                    e.wait_ge(womm.h, m)
                    if lc == 0:
                        e.wait_ge(xsl[pp].h, xl)
                        if t > 0:
                            e.wait_ge(seap.h, sa)
                    e.drain()
                    e.tensor_tensor(resid[:, lc, :], xstm[pp][:, lc, :],
                                    ps_at[lc % 2][:], ALU.add)
                    e.drain()
                    e.reduce_sum(st_s[:, lc:lc + 1], resid[:, lc, :], AX.X
                                 ).then_inc(rsev.h, 1)
                em("vector", f); rsev.inc(1)
                def f(e, lc=lc, m=rsev.n):
                    e.wait_ge(rsev.h, m)
                    e.drain()
                    e.activation(sq_scr[:], resid[:, lc, :], AF.Square,
                                 accum_out=st_q[:, lc:lc + 1]).then_inc(sesq.h, 1)
                em("scalar", f); sesq.inc(1)
            for fn in pend_t2a:
                em("tensor", fn)
            pend_t2a = []
            def f(e, r=rsev.n, q_=sesq.n, sa=seap.n):
                e.wait_ge(rsev.h, r)
                e.wait_ge(sesq.h, q_)
                e.wait_ge(seap.h, sa)
                e.drain()
                e.tensor_scalar_mul(st_m[:], st_s[:], 1.0 / D)
                e.drain()
                e.tensor_scalar_mul(st_v[:], st_q[:], 1.0 / D)
                e.drain()
                e.tensor_tensor(st_b[:], st_m[:], st_m[:], ALU.mult)
                e.drain()
                e.tensor_tensor(st_v[:], st_v[:], st_b[:], ALU.subtract)
                e.drain()
                e.sem_inc(stat.h, 1)
            em("vector", f); stat.inc(1)
            def f(e, s=stat.n):
                e.wait_ge(stat.h, s)
                e.drain()
                e.activation(st_v[:], st_v[:], AF.Sqrt, bias=csts[:, 0:1])
                e.drain()
                e.sem_inc(stat.h, 1)
            em("scalar", f); stat.inc(1)
            def f(e, s=stat.n):
                e.wait_ge(stat.h, s)
                e.drain()
                e.reciprocal(st_r[:], st_v[:])
                e.drain()
                e.scalar_tensor_tensor(st_b[:], st_m[:], -1.0, st_r[:],
                                       ALU.mult, ALU.mult)
                e.drain()
                e.sem_inc(stat.h, 1)
            em("vector", f); stat.inc(1)
            for lc in range(LC2):
                hf = lc % 2
                def f(e, t=t, lc=lc, hf=hf, s=stat.n, lsp=lnsps.n, qv=qx.n):
                    e.wait_ge(stat.h, s)
                    e.wait_ge(qx.h, qv)
                    if lc == 0 and t > 0:
                        e.wait_ge(lnsps.h, lsp)
                    e.drain()
                    e.activation(ln1b[:, lc, :], resid[:, lc, :], AF.Identity,
                                 scale=st_r[:, lc:lc + 1], bias=st_b[:, lc:lc + 1])
                    e.drain()
                    e.activation(aab2[:, hf * 1024:(hf + 1) * 1024], resid[:, lc, :],
                                 AF.Abs, scale=st_r[:, lc:lc + 1],
                                 bias=st_b[:, lc:lc + 1])
                    e.drain()
                    e.activation(sgn2[:, hf * 1024:(hf + 1) * 1024], resid[:, lc, :],
                                 AF.Sign, scale=st_r[:, lc:lc + 1],
                                 bias=st_b[:, lc:lc + 1])
                    e.drain()
                    e.sem_inc(seap.h, 1)
                em("scalar", f); seap.inc(1)
                if hf != 1:
                    continue
                def f(e, t=t, lc=lc, sa=seap.n, xm=xqTm.n):
                    e.wait_ge(seap.h, sa)
                    if xm > 2:
                        e.wait_ge(xqTm.h, xm - 2)
                    jobs = [dict(aab=aab2[:, hh * 1024:(hh + 1) * 1024],
                                 sgn=sgn2[:, hh * 1024:(hh + 1) * 1024],
                                 out=xq_tm[:, lc - 1 + hh, :],
                                 cA=scr2["cA"][:, hh * 1024:(hh + 1) * 1024],
                                 cB=scr2["cB"][:, hh * 1024:(hh + 1) * 1024],
                                 scr_r=scr2["r"][:, hh * 64:(hh + 1) * 64],
                                 scr_h=scr2["h"][:, hh * 64:(hh + 1) * 64],
                                 FD=1024) for hh in range(2)]
                    emit_quant_scales(e, jobs)
                    emit_quant_dense(e, jobs)
                    e.drain()
                    e.sem_inc(qx.h, 1)
                pend_q2a.append(f); qx.inc(1)
                for l2 in (lc - 1, lc):
                    def f(e, l2=l2, q_=qx.n, xe=xqTe.n):
                        e.wait_ge(qx.h, q_)
                        if xe >= 1:
                            e.wait_ge(xqTe.h, xe)
                        last = None
                        for i in range(NKC):
                            last = e.matmul(ps_xqT[:, i * 128:(i + 1) * 128],
                                     xq_tm[:, l2, i * 128:(i + 1) * 128], eye_b[:],
                                     is_transpose=True, start=(i == 0),
                                     stop=(i == NKC - 1))
                        last.then_inc(xqTm.h, 1)
                    pend_t2a.append(f); xqTm.inc(1)
                    def f(e, t=t, l2=l2, m=xqTm.n, xsp=xqsps.n):
                        e.wait_ge(xqTm.h, m)
                        if l2 == 0 and t > 0:
                            e.wait_ge(xqsps.h, xsp)
                        e.drain()
                        e.activation(xq_fm[:, :, l2 * 128:(l2 + 1) * 128],
                                     ps_xqT[:].rearrange("p (i q) -> p i q", q=128),
                                     AF.Identity).then_inc(xqTe.h, 1)
                    pend_e2a.append(f); xqTe.inc(1)
            for fn in pend_e2a:
                em("scalar", fn)
            def f(e, t=t, sa=seap.n):
                e.wait_ge(seap.h, sa)
                e.dma_start(out=ln1_sp[t * LC2:(t + 1) * LC2, :, :]
                            .rearrange("c p d -> p c d"),
                            in_=ln1b[:]).then_inc(lnsps.h, 16)
            em("gpsimd", f); lnsps.inc(16)
            def f(e, t=t, xe=xqTe.n, tok0=tok0):
                e.wait_ge(xqTe.h, xe)
                e.dma_start(out=xq_sp[:, :, tok0:tok0 + TT2]
                            .rearrange("f p s -> p f s"),
                            in_=xq_fm[:]).then_inc(xqsps.h, 16)
            em("gpsimd", f); xqsps.inc(16)

        for fn in pend_q2a:
            em("vector", fn)
        for fn in pend_t2a:
            em("tensor", fn)
        barrier("p2a", [(lnsps, lnsps.n), (xqsps, xqsps.n), (xcvl, xcvl.n),
                        (szvl, szvl.n), (xsl[0], xsl[0].n), (xsl[1], xsl[1].n),
                        (wol, wol.n)])
        mark("p2a")
        p2.close()
        p2ps.close()
        p12.close()

        p3 = ExitStack()
        wqg = sb(p3, "wqg", [128, NKC, IF], BF16)
        wqu = sb(p3, "wqu", [128, NKC, IF], BF16)
        wdT = sb(p3, "wdT", [128, NIC, D], BF16)
        xqt = [sb(p3, f"xqt{i}", [128, NKC, TT3], BF16) for i in range(2)]
        ln1t = sb(p3, "ln1t", [128, LC3, D], BF16)
        sg_s = [sb(p3, f"sg_s{i}", [128, 512], BF16) for i in range(2)]
        h_p = [sb(p3, f"h_p{i}", [128, 1024], F32) for i in range(2)]
        hq_p = [sb(p3, f"hq_p{i}", [128, 1024], BF16) for i in range(2)]
        hq_fm = sb(p3, "hqfm", [128, 1024], BF16)
        aab_p = [sb(p3, f"aab3{i}", [128, 1024], F32) for i in range(2)]
        sgn_p = [sb(p3, f"sgn3{i}", [128, 1024], BF16) for i in range(2)]
        resid2 = sb(p3, "resid2", [128, LC3, D], BF16)
        out_t = [sb(p3, "out_t0", [128, D], F32)]
        sq_scr3 = sb(p3, "sq_scr3", [128, D], BF16)
        t_s = sb(p3, "t3_s", [128, LC3], F32)
        t_q = sb(p3, "t3_q", [128, LC3], F32)
        t_m = sb(p3, "t3_m", [128, LC3], F32)
        t_v = sb(p3, "t3_v", [128, LC3], F32)
        t_r = sb(p3, "t3_r", [128, LC3], F32)
        t_b = sb(p3, "t3_b", [128, LC3], F32)
        scr3 = {k: sb(p3, "s3_" + k, [128, 1024 if k not in ("r", "h") else 64],
                      F32 if k in ("r", "h") else BF16)
                for k in ("r", "h", "cA", "cB")}
        cA_b = [scr3["cA"], scr3["cB"]]
        cBg = [sb(p3, f"cBg{i}", [128, 1024], BF16) for i in range(2)]
        rh_b = [{"r": scr3["r"], "h": scr3["h"]},
                {"r": sb(p3, "s3_r2", [128, 64], F32),
                 "h": sb(p3, "s3_h2", [128, 64], F32)}]

        p3ps = ExitStack()
        ps_g = [p3ps.enter_context(nc.psum_tensor(f"ps_g{i}", [128, 512], F32))
                for i in range(2)]
        ps_u = [p3ps.enter_context(nc.psum_tensor(f"ps_u{i}", [128, 512], F32))
                for i in range(2)]
        ps_dn = p3ps.enter_context(nc.psum_tensor("ps_dn", [128, 1024], F32))
        ps_hqT = p3ps.enter_context(nc.psum_tensor("ps_hqT", [128, 1024], BF16))

        wql = S_("wql"); lnl = S_("lnl")
        xql = [S_("xql0"), S_("xql1")]
        gumm = S_("gumm"); segv = S_("segv"); hev = S_("hev"); seas3 = S_("seas3")
        qh = S_("qh"); qxn = S_("qxn"); qcb = S_("qcb")
        hqTm = S_("hqTm"); hqTe = S_("hqTe"); dnmm = S_("dnmm")
        r2ev = S_("r2ev"); st3 = S_("st3"); seo = S_("seo")
        outs = S_("outs")

        def f(e):
            e.dma_start(out=wqg[:], in_=wqg_sp.rearrange("f p s -> p f s")
                        ).then_inc(wql.h, 16)
            e.dma_start(out=wqu[:], in_=wqu_sp.rearrange("f p s -> p f s")
                        ).then_inc(wql.h, 16)
            e.dma_start(out=wdT[:], in_=wdT_sp.rearrange("f p s -> p f s")
                        ).then_inc(wql.h, 16)
        em("sync", f); wql.inc(48)

        ngrp = 0
        npr = 0
        pend_v = []
        pend_sq = []
        pend_ap = []
        pend_te = []
        pend_odma = []
        for t in range(NT3):
            pp = t % 2
            tok0 = t * TT3
            def f(e, t=t, pp=pp, g=gumm.n, tok0=tok0):
                if t >= 2:
                    e.wait_ge(gumm.h, g - NG * LC3)
                e.dma_start(out=xqt[pp][:],
                            in_=xq_sp[:, :, tok0:tok0 + TT3].rearrange("f p s -> p f s")
                            ).then_inc(xql[pp].h, 16)
            em("sync", f); xql[pp].inc(16)
            def f(e, t=t, r2=r2ev.n):
                if t >= 1:
                    e.wait_ge(r2ev.h, r2)
                e.dma_start(out=ln1t[:],
                            in_=ln1_sp[t * LC3:(t + 1) * LC3, :, :].rearrange(
                                "c p d -> p c d")).then_inc(lnl.h, 16)
            em("sync", f); lnl.inc(16)
            for fn in pend_odma:
                em("sync", fn)
            pend_odma = []
            for lc in range(LC3):
                gu_closures = []
                base = ngrp
                for gi, (ic0, icw) in enumerate(ICG):
                    n = ngrp + gi
                    def f(e, pp=pp, lc=lc, ic0=ic0, icw=icw, n=n, wq=wql.n,
                          xl=xql[pp].n):
                        if n == 0:
                            e.wait_ge(wql.h, wq)
                        if lc == 0 and ic0 == 0:
                            e.wait_ge(xql[pp].h, xl)
                        if n >= 2:
                            e.wait_ge(segv.h, n - 1)
                            e.wait_ge(hev.h, n - 1)
                        last = None
                        for dm in range(NKC):
                            e.matmul(ps_g[n % 2][:, :icw],
                                     xqt[pp][:, dm, lc * 128:(lc + 1) * 128],
                                     wqg[:, dm, ic0:ic0 + icw],
                                     start=(dm == 0), stop=(dm == NKC - 1),
                                     skip_group_check=True)
                            last = e.matmul(ps_u[n % 2][:, :icw],
                                     xqt[pp][:, dm, lc * 128:(lc + 1) * 128],
                                     wqu[:, dm, ic0:ic0 + icw],
                                     start=(dm == 0), stop=(dm == NKC - 1),
                                     skip_group_check=True)
                        last.then_inc(gumm.h, 1)
                    gu_closures.append(f)
                ngrp += NG

                for k in range(NG):
                    em("tensor", gu_closures[k]); gumm.inc(1)
                for fn in pend_te:
                    em("tensor", fn)
                pend_te = []
                se_sil = []; se_abs = []; se_evac = []
                ve_h = []; ve_ladA = []; ve_ladB = []; gp_lad = []
                te_T = []; te_D = []
                for pi, (pc0, pw) in enumerate(ICP):
                    np_ = npr + pi
                    pb = np_ % 2
                    nt_ = pw // 128
                    for hh in range(2):
                        n = base + 2 * pi + hh
                        ic0s, icw = ICG[2 * pi + hh]
                        off = ic0s - pc0
                        def f(e, n=n, icw=icw, m=n + 1, hv=hev.n):
                            e.wait_ge(gumm.h, m)
                            if n >= 2:
                                e.wait_ge(hev.h, hv - 1)
                            e.drain()
                            e.activation(sg_s[n % 2][:, :icw], ps_g[n % 2][:, :icw],
                                         AF.Silu).then_inc(segv.h, 1)
                        se_sil.append(f); segv.inc(1)
                        def f(e, n=n, icw=icw, off=off, pb=pb, np_=np_, m=n + 1,
                              s=segv.n, sa=seas3.n):
                            e.wait_ge(gumm.h, m)
                            e.wait_ge(segv.h, s)
                            if np_ >= 2 and off == 0:
                                e.wait_ge(seas3.h, sa - 1)
                            e.drain()
                            e.tensor_tensor(h_p[pb][:, off:off + icw],
                                            ps_u[n % 2][:, :icw],
                                            sg_s[n % 2][:, :icw], ALU.mult
                                            ).then_inc(hev.h, 1)
                        ve_h.append(f); hev.inc(1)
                    def f(e, pw=pw, pb=pb, np_=np_, hv=hev.n, q_=qh.n):
                        e.wait_ge(hev.h, hv)
                        if np_ >= 2:
                            e.wait_ge(qh.h, q_ - 1)
                        e.drain()
                        e.activation(aab_p[pb][:, :pw], h_p[pb][:, :pw], AF.Abs)
                        e.drain()
                        e.activation(sgn_p[pb][:, :pw], h_p[pb][:, :pw], AF.Sign)
                        e.drain()
                        e.sem_inc(seas3.h, 1)
                    se_abs.append(f); seas3.inc(1)
                    def fA(e, pw=pw, pb=pb, np_=np_, sa=seas3.n, hT=hqTm.n):
                        e.wait_ge(seas3.h, sa)
                        if np_ >= 2:
                            e.wait_ge(hqTm.h, hT - 1)
                        aab = aab_p[pb][:, :pw]
                        emit_quant_scales(e, [dict(aab=aab, scr_r=rh_b[pb]["r"],
                                                   scr_h=rh_b[pb]["h"], FD=pw)])
                        NB = pw // 16
                        aab3 = aab.rearrange("p (nb b) -> p nb b", b=16)
                        rb = rh_b[pb]["r"][:, :NB].unsqueeze(2).broadcast_to(
                            [128, NB, 16])
                        e.tensor_tensor(aab3, aab3, rb, ALU.mult)
                        e.drain()
                        e.sem_inc(qxn.h, 1)
                        cAv = cA_b[pb][:, :pw]
                        cBv = cBg[pb][:, :pw]
                        e.tensor_single_scalar(cAv, aab, 0.25, ALU.is_gt)
                        e.tensor_single_scalar(cBv, aab, 2.5, ALU.is_gt)
                        e.drain()
                        for tA, tB in ((0.75, 3.5), (1.25, 5.0), (1.75, 5.0)):
                            e.scalar_tensor_tensor(cAv, aab, tA, cAv,
                                                   ALU.is_gt, ALU.add)
                            e.scalar_tensor_tensor(cBv, aab, tB, cBv,
                                                   ALU.is_gt, ALU.add)
                            e.drain()
                    ve_ladA.append(fA); qxn.inc(1)
                    def fB(e, pw=pw, pb=pb):
                        cAv = cA_b[pb][:, :pw]
                        cBv = cBg[pb][:, :pw]
                        e.scalar_tensor_tensor(cBv, cBv, 2.0, cAv,
                                               ALU.mult, ALU.add)
                        e.drain()
                        e.tensor_tensor(cBv, cBv, sgn_p[pb][:, :pw], ALU.mult)
                        e.drain()
                        NB = pw // 16
                        hb = rh_b[pb]["h"][:, :NB].unsqueeze(2).broadcast_to(
                            [128, NB, 16])
                        cB3 = cBv.rearrange("p (nb b) -> p nb b", b=16)
                        o3 = hq_p[pb][:, :pw].rearrange("p (nb b) -> p nb b", b=16)
                        e.tensor_tensor(o3, cB3, hb, ALU.mult)
                        e.drain()
                        e.sem_inc(qh.h, 1)
                    ve_ladB.append(fB); qh.inc(1)
                    def f(e, nt_=nt_, pb=pb, q_=qh.n, he_=hqTe.n):
                        e.wait_ge(qh.h, q_)
                        if he_ >= 1:
                            e.wait_ge(hqTe.h, he_)
                        last = None
                        for i in range(nt_):
                            last = e.matmul(ps_hqT[:, i * 128:(i + 1) * 128],
                                     hq_p[pb][:, i * 128:(i + 1) * 128], eye_b[:],
                                     is_transpose=True, start=(i == 0),
                                     stop=(i == nt_ - 1))
                        last.then_inc(hqTm.h, 1)
                    te_T.append(f); hqTm.inc(1)
                    def f(e, nt_=nt_, np_=np_, m=hqTm.n, dn=dnmm.n):
                        e.wait_ge(hqTm.h, m)
                        if np_ >= 1:
                            e.wait_ge(dnmm.h, dn)
                        e.drain()
                        e.activation(hq_fm[:, :nt_ * 128], ps_hqT[:, :nt_ * 128],
                                     AF.Identity).then_inc(hqTe.h, 1)
                    se_evac.append(f); hqTe.inc(1)
                    def f(e, pi=pi, pc0=pc0, nt_=nt_, he_=hqTe.n, r2=r2ev.n):
                        e.wait_ge(hqTe.h, he_)
                        if pi == 0:
                            e.wait_ge(r2ev.h, r2)
                        last = None
                        for i in range(nt_):
                            icg_ = pc0 // 128 + i
                            for half in range(2):
                                last = e.matmul(ps_dn[:, half * 512:(half + 1) * 512],
                                         hq_fm[:, i * 128:(i + 1) * 128],
                                         wdT[:, icg_, half * 512:(half + 1) * 512],
                                         start=(icg_ == 0), stop=(icg_ == NIC - 1),
                                         skip_group_check=True)
                        last.then_inc(dnmm.h, 1)
                    te_D.append(f); dnmm.inc(1)
                em("scalar", se_sil[0]); em("scalar", se_sil[1])
                em("scalar", se_abs[0])
                for fn in pend_sq:
                    em("scalar", fn)
                pend_sq = []
                for fn in (se_sil[2], se_sil[3], se_abs[1], se_evac[0]):
                    em("scalar", fn)
                for fn in pend_ap:
                    em("scalar", fn)
                pend_ap = []
                for fn in (se_sil[4], se_sil[5], se_abs[2], se_evac[1],
                           se_evac[2]):
                    em("scalar", fn)
                for pi in range(len(ICP)):
                    em("tensor", te_T[pi])
                    em("tensor", te_D[pi])
                for fn in gp_lad:
                    em("gpsimd", fn)
                em("vector", ve_h[0]); em("vector", ve_h[1])
                for fn in pend_v:
                    em("vector", fn)
                pend_v = []
                for fn in (ve_h[2], ve_h[3], ve_ladA[0], ve_ladB[0], ve_h[4],
                           ve_h[5], ve_ladA[1], ve_ladB[1], ve_ladA[2],
                           ve_ladB[2]):
                    em("vector", fn)
                npr += len(ICP)
                def f(e, t=t, lc=lc, dn=dnmm.n, ll=lnl.n, sa=seo.n):
                    e.wait_ge(dnmm.h, dn)
                    if lc == 0:
                        e.wait_ge(lnl.h, ll)
                        if t > 0:
                            e.wait_ge(seo.h, sa)
                    e.drain()
                    e.tensor_tensor(resid2[:, lc, :], ln1t[:, lc, :], ps_dn[:],
                                    ALU.add)
                    e.drain()
                    e.reduce_sum(t_s[:, lc:lc + 1], resid2[:, lc, :], AX.X
                                 ).then_inc(r2ev.h, 1)
                r2ev.inc(1)
                pend_v.append(f)
                def f(e, lc=lc, m=r2ev.n):
                    e.wait_ge(r2ev.h, m)
                    e.drain()
                    e.activation(sq_scr3[:], resid2[:, lc, :], AF.Square,
                                 accum_out=t_q[:, lc:lc + 1])
                    e.drain()
                    e.sem_inc(st3.h, 1)
                st3.inc(1)
                pend_sq.append(f)
            def f(e, r=r2ev.n, s=st3.n, so=seo.n):
                e.wait_ge(r2ev.h, r)
                e.wait_ge(st3.h, s)
                e.wait_ge(seo.h, so)
                e.drain()
                e.tensor_scalar_mul(t_m[:], t_s[:], 1.0 / D)
                e.drain()
                e.tensor_scalar_mul(t_v[:], t_q[:], 1.0 / D)
                e.drain()
                e.tensor_tensor(t_b[:], t_m[:], t_m[:], ALU.mult)
                e.drain()
                e.tensor_tensor(t_v[:], t_v[:], t_b[:], ALU.subtract)
                e.drain()
                e.sem_inc(st3.h, 1)
            pend_v.append(f); st3.inc(1)
            def f(e, s=st3.n):
                e.wait_ge(st3.h, s)
                e.drain()
                e.activation(t_v[:], t_v[:], AF.Sqrt, bias=csts[:, 0:1])
                e.drain()
                e.sem_inc(st3.h, 1)
            pend_sq.append(f); st3.inc(1)
            def f(e, s=st3.n):
                e.wait_ge(st3.h, s)
                e.drain()
                e.reciprocal(t_r[:], t_v[:])
                e.drain()
                e.scalar_tensor_tensor(t_b[:], t_m[:], -1.0, t_r[:],
                                       ALU.mult, ALU.mult)
                e.drain()
                e.sem_inc(st3.h, 1)
            pend_v.append(f); st3.inc(1)
            for lc in range(LC3):
                def f(e, t=t, lc=lc, s=st3.n, os_=outs.n):
                    e.wait_ge(st3.h, s)
                    if os_ >= 16:
                        e.wait_ge(outs.h, os_)
                    e.drain()
                    e.activation(out_t[0][:], resid2[:, lc, :],
                                 AF.Identity, scale=t_r[:, lc:lc + 1],
                                 bias=t_b[:, lc:lc + 1])
                    e.drain()
                    e.sem_inc(seo.h, 1)
                pend_ap.append(f); seo.inc(1)
                def f(e, t=t, lc=lc, s=seo.n):
                    e.wait_ge(seo.h, s)
                    e.dma_start(out=out_h[t * TT3 + lc * 128:t * TT3 + (lc + 1) * 128, :],
                                in_=out_t[0][:]).then_inc(outs.h, 16)
                pend_odma.append(f); outs.inc(16)

        for fn in pend_te:
            em("tensor", fn)
        for fn in pend_v:
            em("vector", fn)
        for fn in pend_sq + pend_ap:
            em("scalar", fn)
        for fn in pend_odma:
            em("sync", fn)
        pend_te = []; pend_v = []; pend_sq = []; pend_ap = []
        OUT_TOTAL = outs.n
        for e in ENGS:
            def f(eng, v=OUT_TOTAL):
                eng.wait_ge(outs.h, v)
            em(e, f)

        import os as _os
        _ph = _os.environ.get("ANT_PHASES", "all")
        if _ph != "all":
            cut = marks[_ph]
            for e in ENGS:
                Q[e] = Q[e][:cut[e]]
        with nc.Block() as block:
            @block.sync
            def _(eng):
                for fn in Q["sync"]:
                    fn(eng)
            @block.tensor
            def _(eng):
                for fn in Q["tensor"]:
                    fn(eng)
            @block.vector
            def _(eng):
                for fn in Q["vector"]:
                    fn(eng)
            @block.scalar
            def _(eng):
                for fn in Q["scalar"]:
                    fn(eng)
            @block.gpsimd
            def _(eng):
                for fn in Q["gpsimd"]:
                    fn(eng)
        p3.close()
        p3ps.close()
    return nc


_NC_CACHE = None

def kernel(L0_attn_x, L0_attn_W_in, L0_attn_conv_w, L0_attn_conv_b, L0_attn_W_xproj,
           L0_attn_A_log, L0_attn_D, L0_attn_W_out, L0_ffn_w_gate, L0_ffn_w_up,
           L0_ffn_w_down):
    global _NC_CACHE
    x = np.ascontiguousarray(np.asarray(L0_attn_x, np.float32))
    w_in = np.ascontiguousarray(np.asarray(L0_attn_W_in, np.float32))
    conv_w = np.ascontiguousarray(np.asarray(L0_attn_conv_w, np.float32).reshape(DI, 4))
    conv_b = np.ascontiguousarray(np.asarray(L0_attn_conv_b, np.float32)
                                  .reshape(NDC, 128).T)
    w_xp = np.ascontiguousarray(np.asarray(L0_attn_W_xproj, np.float32))
    dv = np.ascontiguousarray(np.asarray(L0_attn_D, np.float32).reshape(NDC, 128).T)
    w_out = np.ascontiguousarray(np.asarray(L0_attn_W_out, np.float32))
    w_g = np.ascontiguousarray(np.asarray(L0_ffn_w_gate, np.float32))
    w_u = np.ascontiguousarray(np.asarray(L0_ffn_w_up, np.float32))
    w_d = np.ascontiguousarray(np.asarray(L0_ffn_w_down, np.float32))
    eye = np.eye(128, dtype=NPBF16)
    csts = np.zeros((128, 4), np.float32)
    csts[:, 0] = EPS
    csts[:, 1] = 1.0

    if _NC_CACHE is None:
        _NC_CACHE = build_nc()
    nc = _NC_CACHE
    in_maps = []
    for c in range(8):
        in_maps.append({
            "x": np.ascontiguousarray(x[c]),
            "w_in": w_in, "conv_w": conv_w, "conv_b_t": conv_b,
            "w_xproj": w_xp, "d_t": dv, "w_out": w_out,
            "w_gate": w_g, "w_up": w_u, "w_down": w_d,
            "eye": eye, "consts": csts,
        })
    res = run_bass_kernel_spmd(nc, in_maps, core_ids=list(range(8)))
    out = np.stack([np.asarray(res.results[c]["out"], np.float32) for c in range(8)])
    return out


# version 15 (final)
# speedup vs baseline: 1.0162x; 1.0162x over previous
import numpy as np
import ml_dtypes
from contextlib import ExitStack

import concourse.bass as bass
import concourse.mybir as mybir
from concourse.bass_utils import run_bass_kernel_spmd

F32 = mybir.dt.float32
BF16 = mybir.dt.bfloat16
AF = mybir.ActivationFunctionType
ALU = mybir.AluOpType
AX = mybir.AxisListType
NPBF16 = ml_dtypes.bfloat16

T = 8192
D = 1024
DI = 2048
IF = 2816
NKC = D // 128
NDC = DI // 128
NIC = IF // 128
TT1 = 512; NT1 = T // TT1
TT2 = 512; NT2 = T // TT2
TT3 = 512; NT3 = T // TT3
LC1 = TT1 // 128
LC2 = TT2 // 128
LC3 = TT3 // 128
EPS = 1e-5
ICG = [(0, 512), (512, 512), (1024, 512), (1536, 512), (2048, 512), (2560, 256)]
NG = len(ICG)
ICP = [(0, 1024), (1024, 1024), (2048, 768)]
GP_EVERY = 4

ENGS = ("sync", "tensor", "vector", "scalar", "gpsimd")


class S:
    __slots__ = ("h", "n")
    def __init__(self, h):
        self.h = h
        self.n = 0
    def inc(self, k=1):
        self.n += k
        return self.n


def emit_quant_scales(v, jobs):
    v.drain()
    for j in jobs:
        NB = j["FD"] // 16
        a3 = j["aab"].rearrange("p (nb b) -> p nb b", b=16)
        v.tensor_reduce(j["scr_h"][:, :NB], a3, AX.X, ALU.max)
    v.drain()
    for j in jobs:
        NB = j["FD"] // 16
        v.reciprocal(j["scr_r"][:, :NB], j["scr_h"][:, :NB])
    v.drain()
    for j in jobs:
        NB = j["FD"] // 16
        v.tensor_scalar(j["scr_r"][:, :NB], j["scr_r"][:, :NB], 6.0, 6.0e12,
                        ALU.mult, ALU.min)
        v.tensor_scalar(j["scr_h"][:, :NB], j["scr_h"][:, :NB], 1.0 / 12.0, 0.5e-12,
                        ALU.mult, ALU.max)
    v.drain()


def emit_quant_dense(e, jobs):
    for j in jobs:
        NB = j["FD"] // 16
        aab3 = j["aab"].rearrange("p (nb b) -> p nb b", b=16)
        rb = j["scr_r"][:, :NB].unsqueeze(2).broadcast_to([128, NB, 16])
        e.tensor_tensor(aab3, aab3, rb, ALU.mult)
    e.drain()
    for j in jobs:
        af, FD = j["aab"], j["FD"]
        e.tensor_single_scalar(j["cA"][:, :FD], af, 0.25, ALU.is_gt)
        e.tensor_single_scalar(j["cB"][:, :FD], af, 2.5, ALU.is_gt)
    e.drain()
    for tA, tB in ((0.75, 3.5), (1.25, 5.0), (1.75, 5.0)):
        for j in jobs:
            af, FD = j["aab"], j["FD"]
            e.scalar_tensor_tensor(j["cA"][:, :FD], af, tA, j["cA"][:, :FD],
                                   ALU.is_gt, ALU.add)
            e.scalar_tensor_tensor(j["cB"][:, :FD], af, tB, j["cB"][:, :FD],
                                   ALU.is_gt, ALU.add)
        e.drain()
    for j in jobs:
        FD = j["FD"]
        e.scalar_tensor_tensor(j["cB"][:, :FD], j["cB"][:, :FD], 2.0, j["cA"][:, :FD],
                               ALU.mult, ALU.add)
    e.drain()
    for j in jobs:
        FD = j["FD"]
        e.tensor_tensor(j["cB"][:, :FD], j["cB"][:, :FD], j["sgn"], ALU.mult)
    e.drain()
    last = None
    for j in jobs:
        NB = j["FD"] // 16
        hb = j["scr_h"][:, :NB].unsqueeze(2).broadcast_to([128, NB, 16])
        cB3 = j["cB"][:, :j["FD"]].rearrange("p (nb b) -> p nb b", b=16)
        o3 = j["out"].rearrange("p (nb b) -> p nb b", b=16)
        last = e.tensor_tensor(o3, cB3, hb, ALU.mult)
    return last


def build_nc():
    nc = bass.Bass("TRN2", target_bir_lowering=False)

    x_h = nc.dram_tensor("x", [T, D], F32, kind="ExternalInput").ap()
    w_in_h = nc.dram_tensor("w_in", [D, 4 * D], F32, kind="ExternalInput").ap()
    convw_h = nc.dram_tensor("conv_w", [DI, 4], F32, kind="ExternalInput").ap()
    convb_h = nc.dram_tensor("conv_b_t", [128, NDC], F32, kind="ExternalInput").ap()
    wxp_h = nc.dram_tensor("w_xproj", [DI, 33], F32, kind="ExternalInput").ap()
    dvec_h = nc.dram_tensor("d_t", [128, NDC], F32, kind="ExternalInput").ap()
    wout_h = nc.dram_tensor("w_out", [DI, D], F32, kind="ExternalInput").ap()
    wg_h = nc.dram_tensor("w_gate", [IF, D], F32, kind="ExternalInput").ap()
    wu_h = nc.dram_tensor("w_up", [IF, D], F32, kind="ExternalInput").ap()
    wd_h = nc.dram_tensor("w_down", [D, IF], F32, kind="ExternalInput").ap()
    eye_h = nc.dram_tensor("eye", [128, 128], BF16, kind="ExternalInput").ap()
    cst_h = nc.dram_tensor("consts", [128, 4], F32, kind="ExternalInput").ap()
    out_h = nc.dram_tensor("out", [T, D], F32, kind="ExternalOutput").ap()

    xc_sp = nc.dram_tensor("xc_sp", [NDC, 128, T], BF16, kind="Internal").ap()
    sz_sp = nc.dram_tensor("sz_sp", [NDC, 128, T], BF16, kind="Internal").ap()
    xq_sp = nc.dram_tensor("xq_sp", [NKC, 128, T], BF16, kind="Internal").ap()
    ln1_sp = nc.dram_tensor("ln1_sp", [T // 128, 128, D], BF16, kind="Internal").ap()
    win_sp = nc.dram_tensor("win_sp", [32, 128, 1024], BF16, kind="Internal").ap()
    wqg_sp = nc.dram_tensor("wqg_sp", [NKC, 128, IF], BF16, kind="Internal").ap()
    wqu_sp = nc.dram_tensor("wqu_sp", [NKC, 128, IF], BF16, kind="Internal").ap()
    wdT_sp = nc.dram_tensor("wdT_sp", [NIC, 128, D], BF16, kind="Internal").ap()

    Q = {e: [] for e in ENGS}
    marks = {}
    def em(e, fn):
        Q[e].append(fn)
    def mark(name):
        marks[name] = {e: len(Q[e]) for e in ENGS}

    stack = ExitStack()
    sems = {}
    def S_(name):
        if name not in sems:
            sems[name] = S(stack.enter_context(nc.semaphore(name)))
        return sems[name]

    def barrier(tag, dma_waits=()):
        if dma_waits:
            def fw(eng, dw=tuple(dma_waits)):
                for sem, val in dw:
                    eng.wait_ge(sem.h, val)
            em("sync", fw)
        b = S_("bar")
        target = b.n + 5
        for e in ENGS:
            def f(eng, b=b, target=target):
                eng.drain()
                eng.sem_inc(b.h, 1)
                eng.wait_ge(b.h, target)
            em(e, f)
        b.inc(5)

    with stack:
        pctx = stack.enter_context(ExitStack())
        def sb(ctx, name, shape, dt=F32):
            return ctx.enter_context(nc.sbuf_tensor(name, shape, dt))
        eye_b = sb(pctx, "eye_b", [128, 128], BF16)
        csts = sb(pctx, "csts", [128, 4], F32)
        convb_t = sb(pctx, "convb_t", [128, NDC], F32)
        d_t = sb(pctx, "d_t_s", [128, NDC], F32)
        p12 = ExitStack()
        Cm_all = sb(p12, "Cm_all", [16, T], BF16)
        h_acc = sb(p12, "h_acc", [16, DI], F32)

        sld = S_("sld")
        def f(e):
            e.dma_start(out=eye_b[:], in_=eye_h).then_inc(sld.h, 16)
            e.dma_start(out=csts[:], in_=cst_h).then_inc(sld.h, 16)
            e.dma_start(out=convb_t[:], in_=convb_h).then_inc(sld.h, 16)
            e.dma_start(out=d_t[:], in_=dvec_h).then_inc(sld.h, 16)
        em("sync", f); sld.inc(64)
        SMALL_LD = sld.n

        p01 = stack.enter_context(ExitStack())
        diag_b = sb(p01, "diag_b", [128, NDC * 4, 128], BF16)
        wxp_dB = sb(p01, "wxp_dB", [128, NDC, 20], BF16)
        wxp_C = sb(p01, "wxp_C", [128, NDC, 16], BF16)

        p0 = ExitStack()
        wf32 = sb(p0, "wf32", [128, 4 * D], F32)
        wcb = sb(p0, "wcb", [128, 4 * D], BF16)
        convw_s = sb(p0, "convw_s", [128, NDC, 4], F32)
        wxp_st = sb(p0, "wxp_st", [128, NDC, 33], F32)

        wld = S_("wld"); winc = S_("winc"); winsp = S_("winsp")

        def f(e):
            e.dma_start(out=convw_s[:],
                        in_=convw_h.rearrange("(c p) k -> p c k", p=128)
                        ).then_inc(wld.h, 16)
            e.dma_start(out=wxp_st[:].rearrange("p c n -> p c n"),
                        in_=wxp_h.rearrange("(c p) n -> p c n", p=128)
                        ).then_inc(wld.h, 16)
        em("sync", f); wld.inc(32)
        for kc in range(NKC):
            def f(e, kc=kc, c=winc.n):
                if kc > 0:
                    e.wait_ge(winc.h, c)
                e.dma_start(out=wf32[:],
                            in_=w_in_h[kc * 128:(kc + 1) * 128, :]).then_inc(wld.h, 16)
            em("sync", f); wld.inc(16)
            def f(e, kc=kc, w=wld.n, sp=winsp.n):
                e.wait_ge(wld.h, w)
                if kc > 0:
                    e.wait_ge(winsp.h, sp)
                e.drain()
                e.tensor_copy(wcb[:], wf32[:]).then_inc(winc.h, 1)
            em("vector", f); winc.inc(1)
            def f(e, kc=kc, c=winc.n):
                e.wait_ge(winc.h, c)
                e.dma_start(out=win_sp[:, :, kc * 128:(kc + 1) * 128]
                            .rearrange("oc p q -> p oc q"),
                            in_=wcb[:].rearrange("p (oc q) -> p oc q", q=128)
                            ).then_inc(winsp.h, 16)
            em("sync", f); winsp.inc(16)
        def f(e, w=wld.n, s=SMALL_LD):
            e.wait_ge(sld.h, s)
            e.wait_ge(wld.h, w)
            e.drain()
            wxpv = wxp_st[:]
            e.memset(wxp_dB[:], 0.0)
            e.drain()
            e.tensor_copy(wxp_dB[:, :, 0:17], wxpv[:, :, 0:17])
            e.tensor_copy(wxp_C[:], wxpv[:, :, 17:33])
            e.drain()
            for dc in range(NDC):
                for k in range(4):
                    e.tensor_scalar_mul(diag_b[:, dc * 4 + k, :], eye_b[:],
                                        convw_s[:, dc, k:k + 1])
            e.drain()
            e.sem_inc(winc.h, 1)
        em("vector", f); winc.inc(1)

        barrier("p0", [(wld, wld.n), (winsp, winsp.n), (sld, sld.n)])
        mark("p0")
        p0.close()

        p1 = ExitStack()
        winb = sb(p1, "winb", [128, 6, 1024], BF16)
        wst = [sb(p1, f"wst{i}", [128, 1024], F32) for i in range(2)]
        aabw = [sb(p1, f"aabw{i}", [128, 1024], F32) for i in range(2)]
        sgnw = [sb(p1, f"sgnw{i}", [128, 1024], BF16) for i in range(2)]
        wq_tm = [sb(p1, f"wqtm{i}", [128, 1024], BF16) for i in range(4)]
        wqT_st = [sb(p1, f"wqT{i}", [128, NKC, 128], BF16) for i in range(2)]
        scrw = {"r": sb(p1, "sw_r", [128, 64], F32),
                "h": sb(p1, "sw_h", [128, 64], F32),
                "cA": sb(p1, "sw_cA", [128, 1024], BF16),
                "cB": sb(p1, "sw_cB", [128, 1024], BF16)}
        xtm = [sb(p1, f"xtm{i}", [128, LC1, 1024], F32) for i in range(2)]
        xtm_b = sb(p1, "xtmb", [128, LC1, 1024], BF16)
        x_fm = sb(p1, "xfm", [128, NKC, TT1], BF16)
        xp_t = sb(p1, "xp_t", [128, NDC, TT1 + 3], BF16)
        sz_t = sb(p1, "sz_t", [128, NDC, TT1], BF16)
        xc_t = sb(p1, "xc_t", [128, NDC, TT1], BF16)
        xcT = [sb(p1, f"xcT{i}", [128, DI], BF16) for i in range(LC1)]
        ssm_s = sb(p1, "ssm_s", [17, TT1], BF16)
        ssm_tm = sb(p1, "ssm_tm", [128, LC1, 17], F32)
        dB_tm = sb(p1, "dB_tm", [128, LC1, 16], BF16)
        sp1t = sb(p1, "sp1", [1, TT1], F32)
        sp2t = sb(p1, "sp2", [1, TT1], F32)
        sp1 = sp1t[:]
        sp2 = sp2t[:]

        p1ps = ExitStack()
        ps_xz = [p1ps.enter_context(nc.psum_tensor(f"ps_xz{i}", [128, 512], F32))
                 for i in range(2)]
        ps_cv = [p1ps.enter_context(nc.psum_tensor(f"ps_cv{i}", [128, 512], F32))
                 for i in range(2)]
        ps_Ts = [p1ps.enter_context(nc.psum_tensor(f"ps_T{i}", [128, 1024], BF16))
                 for i in range(2)]
        ps_sA = p1ps.enter_context(nc.psum_tensor("ps_sA", [17, TT1], F32))
        ps_hm = p1ps.enter_context(nc.psum_tensor("ps_hm", [16, 512], F32))

        xld = S_("xld"); xcast = S_("xcast"); xfmT = S_("xfmT"); xfme = S_("xfme")
        xzmm = S_("xzmm"); xpev = S_("xpev"); zev = S_("zev"); haloev = S_("haloev")
        cvmm = S_("cvmm"); xcev = S_("xcev"); szsp = S_("szsp"); xcsp = S_("xcsp")
        ssmm = S_("ssmm"); sAev = S_("sAev"); spd = S_("spd"); spadd = S_("spadd")
        sBev = S_("sBev"); sTmm = S_("sTmm"); sTev = S_("sTev"); dBev = S_("dBev")
        xcTmm = S_("xcTmm"); xcTev = S_("xcTev"); hmm = S_("hmm"); hmev = S_("hmev")
        winl = [S_(f"winl{i}") for i in range(6)]
        wldw = [S_("wldw0"), S_("wldw1")]
        wabs = S_("wabs"); wqd = S_("wqd")
        wTm = S_("wTm"); wTev = S_("wTev")
        wspw = [S_("wspw0"), S_("wspw1")]

        psT_free = [None, None]
        psT_idx = [0]
        def psT_next():
            bk = psT_idx[0]
            psT_idx[0] ^= 1
            return bk

        witems = []
        for ch in range(NIC):
            witems.append((wg_h[ch * 128:(ch + 1) * 128, :], 1024, wqg_sp, ch, None))
        for ch in range(NIC):
            witems.append((wu_h[ch * 128:(ch + 1) * 128, :], 1024, wqu_sp, ch, None))
        for dm in range(NKC):
            for (off, w) in ((0, 1024), (1024, 1024), (2048, 768)):
                witems.append((wd_h[dm * 128:(dm + 1) * 128, off:off + w], w,
                               wdT_sp, dm, off))

        def emit_witem_front(j):
            src, W, dst, ch, off = witems[j]
            jb = j % 2
            def f(e, j=j, jb=jb, src=src, W=W, ab=wabs.n):
                if j >= 2:
                    e.wait_ge(wabs.h, ab - 1)
                e.dma_start(out=wst[jb][:, :W], in_=src).then_inc(wldw[jb].h, 16)
            em("gpsimd", f); wldw[jb].inc(16)
            def f(e, j=j, jb=jb, W=W, w=wldw[jb].n):
                e.wait_ge(wldw[jb].h, w)
                if j >= 2:
                    e.wait_ge(wqd.h, j - 1)
                e.drain()
                e.activation(aabw[jb][:, :W], wst[jb][:, :W], AF.Abs)
                e.drain()
                e.activation(sgnw[jb][:, :W], wst[jb][:, :W], AF.Sign)
                e.drain()
                e.sem_inc(wabs.h, 1)
            em("scalar", f); wabs.inc(1)
            def f(e, j=j, jb=jb, W=W, ab=wabs.n):
                e.wait_ge(wabs.h, ab)
                if j >= 4:
                    e.wait_ge(wTm.h, j - 3)
                jobs = [dict(aab=aabw[jb][:, :W], sgn=sgnw[jb][:, :W],
                             out=wq_tm[j % 4][:, :W], cA=scrw["cA"][:, :W],
                             cB=scrw["cB"][:, :W], scr_r=scrw["r"],
                             scr_h=scrw["h"], FD=W)]
                emit_quant_scales(e, jobs)
                emit_quant_dense(e, jobs)
                e.drain()
                e.sem_inc(wqd.h, 1)
            em("vector", f); wqd.inc(1)

        def emit_witem_back(j):
            src, W, dst, ch, off = witems[j]
            nt = W // 128
            jb = j % 2
            bk = psT_next()
            def f(e, j=j, nt=nt, bk=bk, ptf=psT_free[bk]):
                e.wait_ge(wqd.h, j + 1)
                if ptf is not None:
                    e.wait_ge(ptf[0].h, ptf[1])
                last = None
                for i in range(nt):
                    last = e.matmul(ps_Ts[bk][:, i * 128:(i + 1) * 128],
                                    wq_tm[j % 4][:, i * 128:(i + 1) * 128], eye_b[:],
                                    is_transpose=True, start=(i == 0),
                                    stop=(i == nt - 1))
                last.then_inc(wTm.h, 1)
            em("tensor", f); wTm.inc(1)
            def f(e, j=j, jb=jb, nt=nt, bk=bk):
                e.wait_ge(wTm.h, j + 1)
                if j >= 2:
                    e.wait_ge(wspw[jb].h, 16 * (j // 2))
                e.drain()
                e.activation(wqT_st[jb][:, :nt, :],
                             ps_Ts[bk][:, :nt * 128].rearrange("p (i q) -> p i q",
                                                               q=128),
                             AF.Identity).then_inc(wTev.h, 1)
            em("scalar", f); wTev.inc(1)
            psT_free[bk] = (wTev, wTev.n)
            if off is None:
                dhbm = dst[:, :, ch * 128:(ch + 1) * 128]
            else:
                dhbm = dst[off // 128:off // 128 + nt, :, ch * 128:(ch + 1) * 128]
            def f(e, j=j, jb=jb, nt=nt, dhbm=dhbm):
                e.wait_ge(wTev.h, j + 1)
                e.dma_start(out=dhbm.rearrange("f p q -> p f q"),
                            in_=wqT_st[jb][:, :nt, :]).then_inc(wspw[jb].h, 16)
            em("gpsimd", f); wspw[jb].inc(16)

        NWI = len(witems)
        wi_per_tile = [5] * 4 + [4] * 12
        assert sum(wi_per_tile) == NWI
        wi_next = [0]
        wi_done = [0]
        def wi_front(frac, t):
            target = sum(wi_per_tile[:t]) + int(round(wi_per_tile[t] * frac))
            while wi_next[0] < min(target, NWI):
                emit_witem_front(wi_next[0])
                wi_next[0] += 1
        def wi_back(t):
            target = sum(wi_per_tile[:t + 1])
            while wi_done[0] < min(target, wi_next[0]):
                emit_witem_back(wi_done[0])
                wi_done[0] += 1

        def f(e):
            e.memset(h_acc[:], 0.0)
            e.drain()
            e.memset(xp_t[:, :, 0:3], 0.0)
            e.drain()
            e.sem_inc(haloev.h, 1)
        em("vector", f); haloev.inc(1)

        for t in range(NT1):
            pp = t % 2
            xpev0, zev0, xcev0 = xpev.n, zev.n, xcev.n
            def f(e, t=t, xc_=xcast.n):
                if t > 1:
                    e.wait_ge(xcast.h, xc_ - 1)
                e.dma_start(out=xtm[t % 2][:],
                            in_=x_h[t * TT1:(t + 1) * TT1, :].rearrange(
                                "(c p) d -> p c d", p=128)).then_inc(xld.h, 16)
            em("sync", f); xld.inc(16)
            def f(e, t=t, w=xld.n, fT=xfmT.n):
                e.wait_ge(xld.h, w)
                if t >= 1:
                    e.wait_ge(xfmT.h, fT)
                e.drain()
                e.activation(xtm_b[:], xtm[t % 2][:], AF.Identity
                             ).then_inc(xcast.h, 1)
            em("scalar", f); xcast.inc(1)
            for c in range(LC1):
                bk = psT_next()
                def f(e, t=t, c=c, bk=bk, xc_=xcast.n, ptf=psT_free[bk]):
                    if c == 0:
                        e.wait_ge(xcast.h, xc_)
                    if ptf is not None:
                        e.wait_ge(ptf[0].h, ptf[1])
                    last = None
                    for i in range(NKC):
                        last = e.matmul(ps_Ts[bk][:, i * 128:(i + 1) * 128],
                                 xtm_b[:, c, i * 128:(i + 1) * 128], eye_b[:],
                                 is_transpose=True, start=(i == 0), stop=(i == NKC - 1))
                    last.then_inc(xfmT.h, 1)
                em("tensor", f); xfmT.inc(1)
                def f(e, t=t, c=c, bk=bk, m=xfmT.n):
                    e.wait_ge(xfmT.h, m)
                    e.drain()
                    e.activation(x_fm[:, :, c * 128:(c + 1) * 128],
                                 ps_Ts[bk][:].rearrange("p (i q) -> p i q", q=128),
                                 AF.Identity).then_inc(xfme.h, 1)
                em("scalar", f); xfme.inc(1)
                psT_free[bk] = (xfme, xfme.n)
            def f(e, t=t, cm=cvmm.n):
                e.wait_ge(cvmm.h, cm)
                if t > 0:
                    e.drain()
                    e.tensor_copy(xp_t[:, :, 0:3], xp_t[:, :, TT1:TT1 + 3])
                e.drain()
                e.sem_inc(haloev.h, 1)
            em("vector", f); haloev.inc(1)
            for oc in range(32):
                g = 32 * t + oc
                def f(e, g=g, oc=oc):
                    if g >= 5:
                        e.wait_ge(xzmm.h, g - 5)
                    e.dma_start(out=winb[:, g % 6:g % 6 + 1, :],
                                in_=win_sp[oc:oc + 1].rearrange("o p q -> p o q")
                                ).then_inc(winl[g % 6].h, 16)
                em("sync", f); winl[g % 6].inc(16)
            for oc in range(32):
                g = 32 * t + oc
                def f(e, t=t, oc=oc, g=g, fe=xfme.n, xp0=xpev0, zz0=zev0):
                    if oc == 0:
                        e.wait_ge(xfme.h, fe)
                    e.wait_ge(winl[g % 6].h, 16 * (g // 6 + 1))
                    if oc >= 2:
                        if oc - 2 < 16:
                            e.wait_ge(xpev.h, xp0 + oc - 1)
                        else:
                            e.wait_ge(zev.h, zz0 + oc - 17)
                    elif t > 0:
                        e.wait_ge(zev.h, zz0 + oc - 1)
                    last = None
                    for kc in range(NKC):
                        last = e.matmul(ps_xz[oc % 2][:],
                                 winb[:, g % 6, kc * 128:(kc + 1) * 128],
                                 x_fm[:, kc, :],
                                 start=(kc == 0), stop=(kc == NKC - 1))
                    last.then_inc(xzmm.h, 1)
                em("tensor", f); xzmm.inc(1)
                if oc < 16:
                    def f(e, t=t, oc=oc, m=xzmm.n, hl=haloev.n):
                        e.wait_ge(xzmm.h, m)
                        if oc == 0:
                            e.wait_ge(haloev.h, hl)
                        e.drain()
                        e.activation(xp_t[:, oc, 3:TT1 + 3], ps_xz[oc % 2][:],
                                     AF.Identity).then_inc(xpev.h, 1)
                    em("scalar", f); xpev.inc(1)
                else:
                    zc = oc - 16
                    def f(e, t=t, oc=oc, zc=zc, m=xzmm.n, sp_=szsp.n):
                        e.wait_ge(xzmm.h, m)
                        if zc == 0 and t > 0:
                            e.wait_ge(szsp.h, sp_)
                        e.drain()
                        e.activation(sz_t[:, zc, :], ps_xz[oc % 2][:], AF.Silu
                                     ).then_inc(zev.h, 1)
                    em("scalar", f); zev.inc(1)
            wi_front(0.25, t)
            mark(f"p1t{t}xz")
            def f(e, t=t, z=zev.n):
                e.wait_ge(zev.h, z)
                e.dma_start(out=sz_sp[:, :, t * TT1:(t + 1) * TT1]
                            .rearrange("c p s -> p c s"),
                            in_=sz_t[:]).then_inc(szsp.h, 16)
            em("gpsimd", f); szsp.inc(16)
            for dc in range(NDC):
                def f(e, t=t, dc=dc, xp0=xpev0, xc0=xcev0, hl=haloev.n):
                    e.wait_ge(xpev.h, xp0 + dc + 1)
                    if dc == 0:
                        e.wait_ge(haloev.h, hl)
                    if t > 0 or dc >= 2:
                        e.wait_ge(xcev.h, xc0 + dc - 1)
                    last = None
                    for k in range(4):
                        last = e.matmul(ps_cv[dc % 2][:], diag_b[:, dc * 4 + k, :],
                                 xp_t[:, dc, k:k + TT1],
                                 start=(k == 0), stop=(k == 3))
                    last.then_inc(cvmm.h, 1)
                em("tensor", f); cvmm.inc(1)
                def f(e, t=t, dc=dc, m=cvmm.n, sp_=xcsp.n, sm=ssmm.n, xT=xcTmm.n):
                    e.wait_ge(cvmm.h, m)
                    if dc == 0 and t > 0:
                        e.wait_ge(xcsp.h, sp_)
                        e.wait_ge(ssmm.h, sm)
                        e.wait_ge(xcTmm.h, xT)
                    e.drain()
                    e.activation(xc_t[:, dc, :], ps_cv[dc % 2][:], AF.Silu,
                                 bias=convb_t[:, dc:dc + 1]).then_inc(xcev.h, 1)
                em("scalar", f); xcev.inc(1)
            wi_front(0.5, t)
            mark(f"p1t{t}conv")
            def f(e, t=t, xe=xcev.n):
                e.wait_ge(xcev.h, xe)
                e.dma_start(out=xc_sp[:, :, t * TT1:(t + 1) * TT1]
                            .rearrange("c p s -> p c s"),
                            in_=xc_t[:]).then_inc(xcsp.h, 16)
            em("gpsimd", f); xcsp.inc(16)
            mark(f"p1t{t}xcsp")
            def f(e, t=t, xe=xcev.n, ae=sAev.n, sd_=spd.n):
                e.wait_ge(xcev.h, xe)
                if t > 0:
                    e.wait_ge(sAev.h, ae)
                    e.wait_ge(spd.h, sd_)
                last = None
                for dc in range(NDC):
                    last = e.matmul(ps_sA[:], wxp_dB[:, dc, 0:17], xc_t[:, dc, :],
                             start=(dc == 0), stop=(dc == NDC - 1))
                last.then_inc(ssmm.h, 1)
            em("tensor", f); ssmm.inc(1)
            def f(e, m=ssmm.n, st_=sTmm.n, t=t):
                e.wait_ge(ssmm.h, m)
                if t > 0:
                    e.wait_ge(sTmm.h, st_)
                e.drain()
                e.tensor_copy(ssm_s[:], ps_sA[:]).then_inc(sAev.h, 1)
            em("vector", f); sAev.inc(1)
            mark(f"p1t{t}ssmA")
            def f(e, ae=sAev.n):
                e.wait_ge(sAev.h, ae)
                e.drain()
                e.activation(sp1, ssm_s[0:1, :], AF.Abs)
                e.drain()
                e.activation(sp1, sp1, AF.Exp, scale=-1.0)
                e.drain()
                e.activation(sp1, sp1, AF.Ln, bias=csts[0:1, 1:2])
                e.drain()
                e.activation(sp2, ssm_s[0:1, :], AF.Relu)
                e.drain()
                e.sem_inc(spd.h, 1)
            em("scalar", f); spd.inc(1)
            def f(e, s=spd.n):
                e.wait_ge(spd.h, s)
                e.drain()
                e.tensor_tensor(ssm_s[0:1, :], sp1, sp2, ALU.add
                                ).then_inc(spadd.h, 1)
            em("vector", f); spadd.inc(1)
            wi_front(0.75, t)
            mark(f"p1t{t}sp")
            def f(e, t=t, be=sBev.n, hm_=hmev.n):
                if t > 0:
                    e.wait_ge(sBev.h, be)
                    e.wait_ge(hmev.h, hm_)
                last = None
                for dc in range(NDC):
                    last = e.matmul(ps_hm[:], wxp_C[:, dc, :], xc_t[:, dc, :],
                             start=(dc == 0), stop=(dc == NDC - 1))
                last.then_inc(ssmm.h, 1)
            em("tensor", f); ssmm.inc(1)
            def f(e, t=t, m=ssmm.n):
                e.wait_ge(ssmm.h, m)
                e.drain()
                e.tensor_copy(Cm_all[:, t * TT1:(t + 1) * TT1], ps_hm[:]
                              ).then_inc(sBev.h, 1)
            em("vector", f); sBev.inc(1)
            mark(f"p1t{t}ssm")
            bk = psT_next()
            def f(e, bk=bk, ae=sAev.n, sa=spadd.n, ptf=psT_free[bk]):
                e.wait_ge(sAev.h, ae)
                e.wait_ge(spadd.h, sa)
                if ptf is not None:
                    e.wait_ge(ptf[0].h, ptf[1])
                last = None
                for lc in range(LC1):
                    last = e.matmul(ps_Ts[bk][:, lc * 20:lc * 20 + 17],
                             ssm_s[:, lc * 128:(lc + 1) * 128], eye_b[0:17, 0:17],
                             is_transpose=True, start=(lc == 0), stop=(lc == LC1 - 1))
                last.then_inc(sTmm.h, 1)
            em("tensor", f); sTmm.inc(1)
            def f(e, bk=bk, m=sTmm.n):
                e.wait_ge(sTmm.h, m)
                e.drain()
                e.tensor_copy(ssm_tm[:],
                              ps_Ts[bk][:, :LC1 * 20].rearrange("p (c n) -> p c n",
                                                                n=20)
                              [:, :, 0:17]).then_inc(sTev.h, 1)
            em("vector", f); sTev.inc(1)
            psT_free[bk] = (sTev, sTev.n)
            def f(e, m=sTev.n, hm_=hmm.n, t=t):
                e.wait_ge(sTev.h, m)
                if t > 0:
                    e.wait_ge(hmm.h, hm_)
                for lc in range(LC1):
                    e.drain()
                    e.tensor_scalar_mul(dB_tm[:, lc, :], ssm_tm[:, lc, 1:17],
                                        ssm_tm[:, lc, 0:1])
                e.drain()
                e.sem_inc(dBev.h, 1)
            em("vector", f); dBev.inc(1)
            for lc in range(LC1):
                for b in range(2):
                    bk = psT_next()
                    def f(e, t=t, lc=lc, b=b, bk=bk, hm_=hmm.n, ptf=psT_free[bk]):
                        if ptf is not None:
                            e.wait_ge(ptf[0].h, ptf[1])
                        if lc == 0 and b == 0 and t > 0:
                            e.wait_ge(hmm.h, hm_)
                        last = None
                        for i in range(8):
                            dc = b * 8 + i
                            last = e.matmul(ps_Ts[bk][:, i * 128:(i + 1) * 128],
                                     xc_t[:, dc, lc * 128:(lc + 1) * 128], eye_b[:],
                                     is_transpose=True, start=(i == 0), stop=(i == 7))
                        last.then_inc(xcTmm.h, 1)
                    em("tensor", f); xcTmm.inc(1)
                    def f(e, lc=lc, b=b, bk=bk, m=xcTmm.n):
                        e.wait_ge(xcTmm.h, m)
                        e.drain()
                        e.activation(xcT[lc][:, b * 1024:(b + 1) * 1024],
                                     ps_Ts[bk][:], AF.Identity).then_inc(xcTev.h, 1)
                    em("scalar", f); xcTev.inc(1)
                    psT_free[bk] = (xcTev, xcTev.n)
            mark(f"p1t{t}trans")
            for q in range(4):
                def f(e, t=t, q=q, dB=dBev.n, xT=xcTev.n, he=hmev.n, be=sBev.n):
                    if q == 0:
                        e.wait_ge(dBev.h, dB)
                        e.wait_ge(xcTev.h, xT)
                        e.wait_ge(sBev.h, be)
                    if he >= 1:
                        e.wait_ge(hmev.h, he)
                    last = None
                    for lc in range(LC1):
                        last = e.matmul(ps_hm[:], dB_tm[:, lc, :],
                                 xcT[lc][:, q * 512:(q + 1) * 512],
                                 start=(lc == 0), stop=(lc == LC1 - 1))
                    last.then_inc(hmm.h, 1)
                em("tensor", f); hmm.inc(1)
                def f(e, q=q, m=hmm.n):
                    e.wait_ge(hmm.h, m)
                    e.drain()
                    e.tensor_tensor(h_acc[:, q * 512:(q + 1) * 512],
                                    h_acc[:, q * 512:(q + 1) * 512], ps_hm[:],
                                    ALU.add).then_inc(hmev.h, 1)
                em("vector", f); hmev.inc(1)

            wi_front(1.0, t)
            wi_back(t)
            mark(f"p1t{t}")
        assert wi_next[0] == NWI and wi_done[0] == NWI
        barrier("p1", [(szsp, szsp.n), (xcsp, xcsp.n), (xld, xld.n),
                       (wspw[0], wspw[0].n), (wspw[1], wspw[1].n),
                       (wldw[0], wldw[0].n), (wldw[1], wldw[1].n)])
        mark("p1")
        p1.close()
        p1ps.close()
        p01.close()

        p2 = ExitStack()
        wout_b = sb(p2, "wout_b", [128, NDC, D], BF16)
        hmb = sb(p2, "hmb", [16, DI], BF16)
        xcv = sb(p2, "xcv", [128, NDC, TT2], BF16)
        szv = sb(p2, "szv", [128, NDC, TT2], BF16)
        y1 = sb(p2, "y1", [128, NDC, TT2], BF16)
        xstm = [sb(p2, f"xstm{i}", [128, LC2, 1024], F32) for i in range(2)]
        wout_st = xstm[0][:, 0, :]
        resid = sb(p2, "resid", [128, LC2, 1024], F32)
        ln1b = sb(p2, "ln1b", [128, LC2, 1024], BF16)
        xq_tm = sb(p2, "xq_tm", [128, LC2, 1024], BF16)
        xq_fm = sb(p2, "xq_fm", [128, NKC, TT2], BF16)
        sq_scr = sb(p2, "sq_scr", [128, 1024], BF16)
        aab2 = sb(p2, "aab2", [128, 2048], F32)
        sgn2 = sb(p2, "sgn2", [128, 2048], BF16)
        st_s = sb(p2, "st_s", [128, LC2], F32)
        st_q = sb(p2, "st_q", [128, LC2], F32)
        st_m = sb(p2, "st_m", [128, LC2], F32)
        st_v = sb(p2, "st_v", [128, LC2], F32)
        st_r = sb(p2, "st_r", [128, LC2], F32)
        st_b = sb(p2, "st_b", [128, LC2], F32)
        scr2 = {k: sb(p2, "s2_" + k, [128, 2048 if k not in ("r", "h") else 128],
                      F32 if k in ("r", "h") else BF16)
                for k in ("r", "h", "cA", "cB")}

        p2ps = ExitStack()
        ps_y = [p2ps.enter_context(nc.psum_tensor(f"ps_y{i}", [128, TT2], F32))
                for i in range(2)]
        ps_at = [p2ps.enter_context(nc.psum_tensor(f"ps_at{i}", [128, 1024], F32))
                 for i in range(2)]
        ps_xqT = p2ps.enter_context(nc.psum_tensor("ps_xqT", [128, 1024], BF16))

        wol = S_("wol"); woc = S_("woc"); hmc = S_("hmc")
        xcvl = S_("xcvl"); szvl = S_("szvl")
        xsl = [S_("xsl0"), S_("xsl1")]
        ymm = S_("ymm"); ypev = S_("ypev"); womm = S_("womm"); rsev = S_("rsev")
        sesq = S_("sesq"); stat = S_("stat"); seap = S_("seap")
        qx = S_("qx"); xqTm = S_("xqTm"); xqTe = S_("xqTe")
        lnsps = S_("lnsps"); xqsps = S_("xqsps")

        for dc in range(NDC):
            def f(e, dc=dc, c=woc.n):
                if dc > 0:
                    e.wait_ge(woc.h, c)
                e.dma_start(out=wout_st, in_=wout_h[dc * 128:(dc + 1) * 128, :]
                            ).then_inc(wol.h, 16)
            em("sync", f); wol.inc(16)
            def f(e, dc=dc, w=wol.n):
                e.wait_ge(wol.h, w)
                e.drain()
                e.tensor_copy(wout_b[:, dc, :], wout_st).then_inc(woc.h, 1)
            em("vector", f); woc.inc(1)
        def f(e):
            e.drain()
            e.tensor_scalar_mul(hmb[:], h_acc[:], 1.0 / T)
            e.drain()
            e.sem_inc(hmc.h, 1)
        em("vector", f); hmc.inc(1)

        pend_q2a = []
        pend_t2a = []
        for t in range(NT2):
            pend_e2a = []
            pp = t % 2
            tok0 = t * TT2
            ypev0, rsev0 = ypev.n, rsev.n
            def f(e, t=t, yp=ypev.n, tok0=tok0):
                if t >= 1:
                    e.wait_ge(ypev.h, yp)
                e.dma_start(out=xcv[:],
                            in_=xc_sp[:, :, tok0:tok0 + TT2].rearrange("c p s -> p c s")
                            ).then_inc(xcvl.h, 16)
            em("sync", f); xcvl.inc(16)
            def f(e, t=t, yp=ypev.n, tok0=tok0):
                if t >= 1:
                    e.wait_ge(ypev.h, yp)
                e.dma_start(out=szv[:],
                            in_=sz_sp[:, :, tok0:tok0 + TT2].rearrange("c p s -> p c s")
                            ).then_inc(szvl.h, 16)
            em("sync", f); szvl.inc(16)
            def f(e, t=t, pp=pp, rs=rsev.n, wc=woc.n, tok0=tok0):
                if t == 0:
                    e.wait_ge(woc.h, wc)
                if t >= 2:
                    e.wait_ge(rsev.h, rs - LC2)
                e.dma_start(out=xstm[pp][:],
                            in_=x_h[tok0:tok0 + TT2, :].rearrange("(c p) d -> p c d",
                                                                  p=128)
                            ).then_inc(xsl[pp].h, 16)
            em("sync", f); xsl[pp].inc(16)
            for dc in range(NDC):
                def f(e, t=t, dc=dc, hc=hmc.n, yp0=ypev0, tok0=tok0):
                    if t == 0 and dc == 0:
                        e.wait_ge(hmc.h, hc)
                    if t > 0 or dc >= 2:
                        e.wait_ge(ypev.h, yp0 + dc - 1)
                    e.matmul(ps_y[dc % 2][:], hmb[:, dc * 128:(dc + 1) * 128],
                             Cm_all[:, tok0:tok0 + TT2], start=True, stop=True
                             ).then_inc(ymm.h, 1)
                em("tensor", f); ymm.inc(1)
                def f(e, t=t, pp=pp, dc=dc, m=ymm.n, xl=xcvl.n, zl=szvl.n, wm=womm.n):
                    e.wait_ge(ymm.h, m)
                    if dc == 0:
                        e.wait_ge(xcvl.h, xl)
                        e.wait_ge(szvl.h, zl)
                        if t > 0:
                            e.wait_ge(womm.h, wm)
                    e.drain()
                    e.scalar_tensor_tensor(y1[:, dc, :], xcv[:, dc, :], d_t[:, dc:dc + 1],
                                           ps_y[dc % 2][:], ALU.mult, ALU.add)
                    e.drain()
                    e.tensor_tensor(y1[:, dc, :], y1[:, dc, :], szv[:, dc, :], ALU.mult
                                    ).then_inc(ypev.h, 1)
                em("vector", f); ypev.inc(1)
            for fn in pend_q2a:
                em("vector", fn)
            pend_q2a = []
            for lc in range(LC2):
                def f(e, t=t, lc=lc, yp=ypev.n, rs0=rsev0):
                    if lc == 0:
                        e.wait_ge(ypev.h, yp)
                    if t > 0 or lc >= 2:
                        e.wait_ge(rsev.h, rs0 + lc - 1)
                    last = None
                    for half in range(2):
                        for dc in range(NDC):
                            last = e.matmul(ps_at[lc % 2][:, half * 512:(half + 1) * 512],
                                     y1[:, dc, lc * 128:(lc + 1) * 128],
                                     wout_b[:, dc, half * 512:(half + 1) * 512],
                                     start=(dc == 0), stop=(dc == NDC - 1))
                    last.then_inc(womm.h, 1)
                em("tensor", f); womm.inc(1)
                def f(e, t=t, pp=pp, lc=lc, m=womm.n, xl=xsl[pp].n, sa=seap.n):
                    e.wait_ge(womm.h, m)
                    if lc == 0:
                        e.wait_ge(xsl[pp].h, xl)
                        if t > 0:
                            e.wait_ge(seap.h, sa)
                    e.drain()
                    e.scalar_tensor_tensor(resid[:, lc, :], ps_at[lc % 2][:], 1.0,
                                           xstm[pp][:, lc, :],
                                           ALU.mult, ALU.add)
                    e.drain()
                    e.reduce_sum(st_s[:, lc:lc + 1], resid[:, lc, :], AX.X
                                 ).then_inc(rsev.h, 1)
                em("vector", f); rsev.inc(1)
                def f(e, lc=lc, m=rsev.n):
                    e.wait_ge(rsev.h, m)
                    e.drain()
                    e.activation(sq_scr[:], resid[:, lc, :], AF.Square,
                                 accum_out=st_q[:, lc:lc + 1]).then_inc(sesq.h, 1)
                em("scalar", f); sesq.inc(1)
            for fn in pend_t2a:
                em("tensor", fn)
            pend_t2a = []
            def f(e, r=rsev.n, q_=sesq.n, sa=seap.n):
                e.wait_ge(rsev.h, r)
                e.wait_ge(sesq.h, q_)
                e.wait_ge(seap.h, sa)
                e.drain()
                e.tensor_scalar_mul(st_m[:], st_s[:], 1.0 / D)
                e.drain()
                e.tensor_scalar_mul(st_v[:], st_q[:], 1.0 / D)
                e.drain()
                e.tensor_tensor(st_b[:], st_m[:], st_m[:], ALU.mult)
                e.drain()
                e.tensor_tensor(st_v[:], st_v[:], st_b[:], ALU.subtract)
                e.drain()
                e.sem_inc(stat.h, 1)
            em("vector", f); stat.inc(1)
            def f(e, s=stat.n):
                e.wait_ge(stat.h, s)
                e.drain()
                e.activation(st_v[:], st_v[:], AF.Sqrt, bias=csts[:, 0:1])
                e.drain()
                e.sem_inc(stat.h, 1)
            em("scalar", f); stat.inc(1)
            def f(e, s=stat.n):
                e.wait_ge(stat.h, s)
                e.drain()
                e.reciprocal(st_r[:], st_v[:])
                e.drain()
                e.scalar_tensor_tensor(st_b[:], st_m[:], -1.0, st_r[:],
                                       ALU.mult, ALU.mult)
                e.drain()
                e.sem_inc(stat.h, 1)
            em("vector", f); stat.inc(1)
            for lc in range(LC2):
                hf = lc % 2
                def f(e, t=t, lc=lc, hf=hf, s=stat.n, lsp=lnsps.n, qv=qx.n):
                    e.wait_ge(stat.h, s)
                    e.wait_ge(qx.h, qv)
                    if lc == 0 and t > 0:
                        e.wait_ge(lnsps.h, lsp)
                    e.drain()
                    e.activation(ln1b[:, lc, :], resid[:, lc, :], AF.Identity,
                                 scale=st_r[:, lc:lc + 1], bias=st_b[:, lc:lc + 1])
                    e.drain()
                    e.activation(aab2[:, hf * 1024:(hf + 1) * 1024], resid[:, lc, :],
                                 AF.Abs, scale=st_r[:, lc:lc + 1],
                                 bias=st_b[:, lc:lc + 1])
                    e.drain()
                    e.activation(sgn2[:, hf * 1024:(hf + 1) * 1024], resid[:, lc, :],
                                 AF.Sign, scale=st_r[:, lc:lc + 1],
                                 bias=st_b[:, lc:lc + 1])
                    e.drain()
                    e.sem_inc(seap.h, 1)
                em("scalar", f); seap.inc(1)
                if hf != 1:
                    continue
                def f(e, t=t, lc=lc, sa=seap.n, xm=xqTm.n):
                    e.wait_ge(seap.h, sa)
                    if xm > 2:
                        e.wait_ge(xqTm.h, xm - 2)
                    jobs = [dict(aab=aab2[:, hh * 1024:(hh + 1) * 1024],
                                 sgn=sgn2[:, hh * 1024:(hh + 1) * 1024],
                                 out=xq_tm[:, lc - 1 + hh, :],
                                 cA=scr2["cA"][:, hh * 1024:(hh + 1) * 1024],
                                 cB=scr2["cB"][:, hh * 1024:(hh + 1) * 1024],
                                 scr_r=scr2["r"][:, hh * 64:(hh + 1) * 64],
                                 scr_h=scr2["h"][:, hh * 64:(hh + 1) * 64],
                                 FD=1024) for hh in range(2)]
                    emit_quant_scales(e, jobs)
                    emit_quant_dense(e, jobs)
                    e.drain()
                    e.sem_inc(qx.h, 1)
                pend_q2a.append(f); qx.inc(1)
                for l2 in (lc - 1, lc):
                    def f(e, l2=l2, q_=qx.n, xe=xqTe.n):
                        e.wait_ge(qx.h, q_)
                        if xe >= 1:
                            e.wait_ge(xqTe.h, xe)
                        last = None
                        for i in range(NKC):
                            last = e.matmul(ps_xqT[:, i * 128:(i + 1) * 128],
                                     xq_tm[:, l2, i * 128:(i + 1) * 128], eye_b[:],
                                     is_transpose=True, start=(i == 0),
                                     stop=(i == NKC - 1))
                        last.then_inc(xqTm.h, 1)
                    pend_t2a.append(f); xqTm.inc(1)
                    def f(e, t=t, l2=l2, m=xqTm.n, xsp=xqsps.n):
                        e.wait_ge(xqTm.h, m)
                        if l2 == 0 and t > 0:
                            e.wait_ge(xqsps.h, xsp)
                        e.drain()
                        e.activation(xq_fm[:, :, l2 * 128:(l2 + 1) * 128],
                                     ps_xqT[:].rearrange("p (i q) -> p i q", q=128),
                                     AF.Identity).then_inc(xqTe.h, 1)
                    pend_e2a.append(f); xqTe.inc(1)
            for fn in pend_e2a:
                em("scalar", fn)
            def f(e, t=t, sa=seap.n):
                e.wait_ge(seap.h, sa)
                e.dma_start(out=ln1_sp[t * LC2:(t + 1) * LC2, :, :]
                            .rearrange("c p d -> p c d"),
                            in_=ln1b[:]).then_inc(lnsps.h, 16)
            em("gpsimd", f); lnsps.inc(16)
            def f(e, t=t, xe=xqTe.n, tok0=tok0):
                e.wait_ge(xqTe.h, xe)
                e.dma_start(out=xq_sp[:, :, tok0:tok0 + TT2]
                            .rearrange("f p s -> p f s"),
                            in_=xq_fm[:]).then_inc(xqsps.h, 16)
            em("gpsimd", f); xqsps.inc(16)

        for fn in pend_q2a:
            em("vector", fn)
        for fn in pend_t2a:
            em("tensor", fn)
        barrier("p2a", [(lnsps, lnsps.n), (xqsps, xqsps.n), (xcvl, xcvl.n),
                        (szvl, szvl.n), (xsl[0], xsl[0].n), (xsl[1], xsl[1].n),
                        (wol, wol.n)])
        mark("p2a")
        p2.close()
        p2ps.close()
        p12.close()

        p3 = ExitStack()
        wqg = sb(p3, "wqg", [128, NKC, IF], BF16)
        wqu = sb(p3, "wqu", [128, NKC, IF], BF16)
        wdT = sb(p3, "wdT", [128, NIC, D], BF16)
        xqt = [sb(p3, f"xqt{i}", [128, NKC, TT3], BF16) for i in range(2)]
        ln1t = sb(p3, "ln1t", [128, LC3, D], BF16)
        sg_s = [sb(p3, f"sg_s{i}", [128, 512], BF16) for i in range(2)]
        h_p = [sb(p3, f"h_p{i}", [128, 1024], F32) for i in range(2)]
        hq_p = [sb(p3, f"hq_p{i}", [128, 1024], BF16) for i in range(2)]
        hq_fm = sb(p3, "hqfm", [128, 1024], BF16)
        aab_p = [sb(p3, f"aab3{i}", [128, 1024], F32) for i in range(2)]
        sgn_p = [sb(p3, f"sgn3{i}", [128, 1024], BF16) for i in range(2)]
        resid2 = sb(p3, "resid2", [128, LC3, D], BF16)
        out_t = [sb(p3, f"out_t{i}", [128, D], F32) for i in range(2)]
        sq_scr3 = sb(p3, "sq_scr3", [128, D], BF16)
        t_s = sb(p3, "t3_s", [128, LC3], F32)
        t_q = sb(p3, "t3_q", [128, LC3], F32)
        t_m = sb(p3, "t3_m", [128, LC3], F32)
        t_v = sb(p3, "t3_v", [128, LC3], F32)
        t_r = sb(p3, "t3_r", [128, LC3], F32)
        t_b = sb(p3, "t3_b", [128, LC3], F32)
        scr3 = {k: sb(p3, "s3_" + k, [128, 1024 if k not in ("r", "h") else 64],
                      F32 if k in ("r", "h") else BF16)
                for k in ("r", "h", "cA", "cB")}

        p3ps = ExitStack()
        ps_g = [p3ps.enter_context(nc.psum_tensor(f"ps_g{i}", [128, 512], F32))
                for i in range(2)]
        ps_u = [p3ps.enter_context(nc.psum_tensor(f"ps_u{i}", [128, 512], F32))
                for i in range(2)]
        ps_dn = p3ps.enter_context(nc.psum_tensor("ps_dn", [128, 1024], F32))
        ps_hqT = p3ps.enter_context(nc.psum_tensor("ps_hqT", [128, 1024], BF16))

        wql = S_("wql"); lnl = S_("lnl")
        xql = [S_("xql0"), S_("xql1")]
        gumm = S_("gumm"); segv = S_("segv"); hev = S_("hev"); seas3 = S_("seas3")
        qh = S_("qh")
        hqTm = S_("hqTm"); hqTe = S_("hqTe"); dnmm = S_("dnmm")
        r2ev = S_("r2ev"); st3 = S_("st3"); seo = S_("seo")
        outs = [S_("outs0"), S_("outs1")]

        def f(e):
            e.dma_start(out=wqg[:], in_=wqg_sp.rearrange("f p s -> p f s")
                        ).then_inc(wql.h, 16)
            e.dma_start(out=wqu[:], in_=wqu_sp.rearrange("f p s -> p f s")
                        ).then_inc(wql.h, 16)
            e.dma_start(out=wdT[:], in_=wdT_sp.rearrange("f p s -> p f s")
                        ).then_inc(wql.h, 16)
        em("sync", f); wql.inc(48)

        ngrp = 0
        npr = 0
        pend_v = []
        pend_sq = []
        pend_ap = []
        pend_te = []
        for t in range(NT3):
            pp = t % 2
            tok0 = t * TT3
            def f(e, t=t, pp=pp, g=gumm.n, tok0=tok0):
                if t >= 2:
                    e.wait_ge(gumm.h, g - NG * LC3)
                e.dma_start(out=xqt[pp][:],
                            in_=xq_sp[:, :, tok0:tok0 + TT3].rearrange("f p s -> p f s")
                            ).then_inc(xql[pp].h, 16)
            em("sync", f); xql[pp].inc(16)
            def f(e, t=t, r2=r2ev.n):
                if t >= 1:
                    e.wait_ge(r2ev.h, r2)
                e.dma_start(out=ln1t[:],
                            in_=ln1_sp[t * LC3:(t + 1) * LC3, :, :].rearrange(
                                "c p d -> p c d")).then_inc(lnl.h, 16)
            em("sync", f); lnl.inc(16)
            for lc in range(LC3):
                gu_closures = []
                base = ngrp
                for gi, (ic0, icw) in enumerate(ICG):
                    n = ngrp + gi
                    def f(e, pp=pp, lc=lc, ic0=ic0, icw=icw, n=n, wq=wql.n,
                          xl=xql[pp].n):
                        if n == 0:
                            e.wait_ge(wql.h, wq)
                        if lc == 0 and ic0 == 0:
                            e.wait_ge(xql[pp].h, xl)
                        if n >= 2:
                            e.wait_ge(segv.h, n - 1)
                            e.wait_ge(hev.h, n - 1)
                        last = None
                        for dm in range(NKC):
                            e.matmul(ps_g[n % 2][:, :icw],
                                     xqt[pp][:, dm, lc * 128:(lc + 1) * 128],
                                     wqg[:, dm, ic0:ic0 + icw],
                                     start=(dm == 0), stop=(dm == NKC - 1),
                                     skip_group_check=True)
                            last = e.matmul(ps_u[n % 2][:, :icw],
                                     xqt[pp][:, dm, lc * 128:(lc + 1) * 128],
                                     wqu[:, dm, ic0:ic0 + icw],
                                     start=(dm == 0), stop=(dm == NKC - 1),
                                     skip_group_check=True)
                        last.then_inc(gumm.h, 1)
                    gu_closures.append(f)
                ngrp += NG

                for k in range(NG):
                    em("tensor", gu_closures[k]); gumm.inc(1)
                for fn in pend_te:
                    em("tensor", fn)
                pend_te = []
                se_sil = []; se_abs = []; se_evac = []
                ve_h = []; ve_lad = []
                te_T = []; te_D = []
                for pi, (pc0, pw) in enumerate(ICP):
                    np_ = npr + pi
                    pb = np_ % 2
                    nt_ = pw // 128
                    for hh in range(2):
                        n = base + 2 * pi + hh
                        ic0s, icw = ICG[2 * pi + hh]
                        off = ic0s - pc0
                        def f(e, n=n, icw=icw, m=n + 1, hv=hev.n):
                            e.wait_ge(gumm.h, m)
                            if n >= 2:
                                e.wait_ge(hev.h, hv - 1)
                            e.drain()
                            e.activation(sg_s[n % 2][:, :icw], ps_g[n % 2][:, :icw],
                                         AF.Silu).then_inc(segv.h, 1)
                        se_sil.append(f); segv.inc(1)
                        def f(e, n=n, icw=icw, off=off, pb=pb, np_=np_, m=n + 1,
                              s=segv.n, sa=seas3.n):
                            e.wait_ge(gumm.h, m)
                            e.wait_ge(segv.h, s)
                            if np_ >= 2 and off == 0:
                                e.wait_ge(seas3.h, sa - 1)
                            e.drain()
                            e.scalar_tensor_tensor(h_p[pb][:, off:off + icw],
                                                   ps_u[n % 2][:, :icw], 1.0,
                                                   sg_s[n % 2][:, :icw],
                                                   ALU.mult, ALU.mult
                                                   ).then_inc(hev.h, 1)
                        ve_h.append(f); hev.inc(1)
                    def f(e, pw=pw, pb=pb, np_=np_, hv=hev.n, q_=qh.n):
                        e.wait_ge(hev.h, hv)
                        if np_ >= 2:
                            e.wait_ge(qh.h, q_ - 1)
                        e.drain()
                        e.activation(aab_p[pb][:, :pw], h_p[pb][:, :pw], AF.Abs)
                        e.drain()
                        e.activation(sgn_p[pb][:, :pw], h_p[pb][:, :pw], AF.Sign)
                        e.drain()
                        e.sem_inc(seas3.h, 1)
                    se_abs.append(f); seas3.inc(1)
                    def f(e, pw=pw, pb=pb, np_=np_, sa=seas3.n, hT=hqTm.n):
                        e.wait_ge(seas3.h, sa)
                        if np_ >= 2:
                            e.wait_ge(hqTm.h, hT - 1)
                        jobs = [dict(aab=aab_p[pb][:, :pw], sgn=sgn_p[pb][:, :pw],
                                     out=hq_p[pb][:, :pw], cA=scr3["cA"][:, :pw],
                                     cB=scr3["cB"][:, :pw], scr_r=scr3["r"],
                                     scr_h=scr3["h"], FD=pw)]
                        emit_quant_scales(e, jobs)
                        emit_quant_dense(e, jobs)
                        e.drain()
                        e.sem_inc(qh.h, 1)
                    ve_lad.append(f); qh.inc(1)
                    def f(e, nt_=nt_, pb=pb, q_=qh.n, he_=hqTe.n):
                        e.wait_ge(qh.h, q_)
                        if he_ >= 1:
                            e.wait_ge(hqTe.h, he_)
                        last = None
                        for i in range(nt_):
                            last = e.matmul(ps_hqT[:, i * 128:(i + 1) * 128],
                                     hq_p[pb][:, i * 128:(i + 1) * 128], eye_b[:],
                                     is_transpose=True, start=(i == 0),
                                     stop=(i == nt_ - 1))
                        last.then_inc(hqTm.h, 1)
                    te_T.append(f); hqTm.inc(1)
                    def f(e, nt_=nt_, np_=np_, m=hqTm.n, dn=dnmm.n):
                        e.wait_ge(hqTm.h, m)
                        if np_ >= 1:
                            e.wait_ge(dnmm.h, dn)
                        e.drain()
                        e.activation(hq_fm[:, :nt_ * 128], ps_hqT[:, :nt_ * 128],
                                     AF.Identity).then_inc(hqTe.h, 1)
                    se_evac.append(f); hqTe.inc(1)
                    def f(e, pi=pi, pc0=pc0, nt_=nt_, he_=hqTe.n, r2=r2ev.n):
                        e.wait_ge(hqTe.h, he_)
                        if pi == 0:
                            e.wait_ge(r2ev.h, r2)
                        last = None
                        for i in range(nt_):
                            icg_ = pc0 // 128 + i
                            for half in range(2):
                                last = e.matmul(ps_dn[:, half * 512:(half + 1) * 512],
                                         hq_fm[:, i * 128:(i + 1) * 128],
                                         wdT[:, icg_, half * 512:(half + 1) * 512],
                                         start=(icg_ == 0), stop=(icg_ == NIC - 1),
                                         skip_group_check=True)
                        last.then_inc(dnmm.h, 1)
                    te_D.append(f); dnmm.inc(1)
                em("scalar", se_sil[0]); em("scalar", se_sil[1])
                em("scalar", se_abs[0])
                for fn in pend_sq:
                    em("scalar", fn)
                pend_sq = []
                for fn in (se_sil[2], se_sil[3], se_abs[1], se_evac[0]):
                    em("scalar", fn)
                for fn in pend_ap:
                    em("scalar", fn)
                pend_ap = []
                for fn in (se_sil[4], se_sil[5], se_abs[2], se_evac[1],
                           se_evac[2]):
                    em("scalar", fn)
                for pi in range(len(ICP)):
                    em("tensor", te_T[pi])
                    em("tensor", te_D[pi])
                em("vector", ve_h[0]); em("vector", ve_h[1])
                for fn in pend_v:
                    em("vector", fn)
                pend_v = []
                for fn in (ve_h[2], ve_h[3], ve_lad[0], ve_h[4], ve_h[5],
                           ve_lad[1], ve_lad[2]):
                    em("vector", fn)
                npr += len(ICP)
                def f(e, t=t, lc=lc, dn=dnmm.n, ll=lnl.n, sa=seo.n):
                    e.wait_ge(dnmm.h, dn)
                    if lc == 0:
                        e.wait_ge(lnl.h, ll)
                        if t > 0:
                            e.wait_ge(seo.h, sa)
                    e.drain()
                    e.scalar_tensor_tensor(resid2[:, lc, :], ps_dn[:], 1.0,
                                           ln1t[:, lc, :], ALU.mult, ALU.add)
                    e.drain()
                    e.reduce_sum(t_s[:, lc:lc + 1], resid2[:, lc, :], AX.X
                                 ).then_inc(r2ev.h, 1)
                r2ev.inc(1)
                pend_v.append(f)
                def f(e, lc=lc, m=r2ev.n):
                    e.wait_ge(r2ev.h, m)
                    e.drain()
                    e.activation(sq_scr3[:], resid2[:, lc, :], AF.Square,
                                 accum_out=t_q[:, lc:lc + 1])
                    e.drain()
                    e.sem_inc(st3.h, 1)
                st3.inc(1)
                pend_sq.append(f)
            def f(e, r=r2ev.n, s=st3.n, so=seo.n):
                e.wait_ge(r2ev.h, r)
                e.wait_ge(st3.h, s)
                e.wait_ge(seo.h, so)
                e.drain()
                e.tensor_scalar_mul(t_m[:], t_s[:], 1.0 / D)
                e.drain()
                e.tensor_scalar_mul(t_v[:], t_q[:], 1.0 / D)
                e.drain()
                e.tensor_tensor(t_b[:], t_m[:], t_m[:], ALU.mult)
                e.drain()
                e.tensor_tensor(t_v[:], t_v[:], t_b[:], ALU.subtract)
                e.drain()
                e.sem_inc(st3.h, 1)
            pend_v.append(f); st3.inc(1)
            def f(e, s=st3.n):
                e.wait_ge(st3.h, s)
                e.drain()
                e.activation(t_v[:], t_v[:], AF.Sqrt, bias=csts[:, 0:1])
                e.drain()
                e.sem_inc(st3.h, 1)
            pend_sq.append(f); st3.inc(1)
            def f(e, s=st3.n):
                e.wait_ge(st3.h, s)
                e.drain()
                e.reciprocal(t_r[:], t_v[:])
                e.drain()
                e.scalar_tensor_tensor(t_b[:], t_m[:], -1.0, t_r[:],
                                       ALU.mult, ALU.mult)
                e.drain()
                e.sem_inc(st3.h, 1)
            pend_v.append(f); st3.inc(1)
            for lc in range(LC3):
                k = t * LC3 + lc
                def f(e, t=t, lc=lc, k=k, s=st3.n, os_=outs[k % 2].n):
                    e.wait_ge(st3.h, s)
                    if os_ >= 16:
                        e.wait_ge(outs[k % 2].h, os_)
                    e.drain()
                    e.activation(out_t[k % 2][:], resid2[:, lc, :],
                                 AF.Identity, scale=t_r[:, lc:lc + 1],
                                 bias=t_b[:, lc:lc + 1])
                    e.drain()
                    e.sem_inc(seo.h, 1)
                pend_ap.append(f); seo.inc(1)
                def f(e, t=t, lc=lc, k=k, s=seo.n):
                    e.wait_ge(seo.h, s)
                    e.dma_start(out=out_h[t * TT3 + lc * 128:t * TT3 + (lc + 1) * 128, :],
                                in_=out_t[k % 2][:]).then_inc(outs[k % 2].h, 16)
                em("gpsimd", f); outs[k % 2].inc(16)

        for fn in pend_te:
            em("tensor", fn)
        for fn in pend_v:
            em("vector", fn)
        for fn in pend_sq + pend_ap:
            em("scalar", fn)
        pend_te = []; pend_v = []; pend_sq = []; pend_ap = []
        OT0, OT1 = outs[0].n, outs[1].n
        for e in ENGS:
            def f(eng, v0=OT0, v1=OT1):
                eng.wait_ge(outs[0].h, v0)
                eng.wait_ge(outs[1].h, v1)
            em(e, f)

        import os as _os
        _ph = _os.environ.get("ANT_PHASES", "all")
        if _ph != "all":
            cut = marks[_ph]
            for e in ENGS:
                Q[e] = Q[e][:cut[e]]
        with nc.Block() as block:
            @block.sync
            def _(eng):
                for fn in Q["sync"]:
                    fn(eng)
            @block.tensor
            def _(eng):
                for fn in Q["tensor"]:
                    fn(eng)
            @block.vector
            def _(eng):
                for fn in Q["vector"]:
                    fn(eng)
            @block.scalar
            def _(eng):
                for fn in Q["scalar"]:
                    fn(eng)
            @block.gpsimd
            def _(eng):
                for fn in Q["gpsimd"]:
                    fn(eng)
        p3.close()
        p3ps.close()
    return nc


_NC_CACHE = None

def kernel(L0_attn_x, L0_attn_W_in, L0_attn_conv_w, L0_attn_conv_b, L0_attn_W_xproj,
           L0_attn_A_log, L0_attn_D, L0_attn_W_out, L0_ffn_w_gate, L0_ffn_w_up,
           L0_ffn_w_down):
    global _NC_CACHE
    x = np.ascontiguousarray(np.asarray(L0_attn_x, np.float32))
    w_in = np.ascontiguousarray(np.asarray(L0_attn_W_in, np.float32))
    conv_w = np.ascontiguousarray(np.asarray(L0_attn_conv_w, np.float32).reshape(DI, 4))
    conv_b = np.ascontiguousarray(np.asarray(L0_attn_conv_b, np.float32)
                                  .reshape(NDC, 128).T)
    w_xp = np.ascontiguousarray(np.asarray(L0_attn_W_xproj, np.float32))
    dv = np.ascontiguousarray(np.asarray(L0_attn_D, np.float32).reshape(NDC, 128).T)
    w_out = np.ascontiguousarray(np.asarray(L0_attn_W_out, np.float32))
    w_g = np.ascontiguousarray(np.asarray(L0_ffn_w_gate, np.float32))
    w_u = np.ascontiguousarray(np.asarray(L0_ffn_w_up, np.float32))
    w_d = np.ascontiguousarray(np.asarray(L0_ffn_w_down, np.float32))
    eye = np.eye(128, dtype=NPBF16)
    csts = np.zeros((128, 4), np.float32)
    csts[:, 0] = EPS
    csts[:, 1] = 1.0

    if _NC_CACHE is None:
        _NC_CACHE = build_nc()
    nc = _NC_CACHE
    in_maps = []
    for c in range(8):
        in_maps.append({
            "x": np.ascontiguousarray(x[c]),
            "w_in": w_in, "conv_w": conv_w, "conv_b_t": conv_b,
            "w_xproj": w_xp, "d_t": dv, "w_out": w_out,
            "w_gate": w_g, "w_up": w_u, "w_down": w_d,
            "eye": eye, "consts": csts,
        })
    res = run_bass_kernel_spmd(nc, in_maps, core_ids=list(range(8)))
    out = np.stack([np.asarray(res.results[c]["out"], np.float32) for c in range(8)])
    return out
